# version 20
# baseline (speedup 1.0000x reference)
import numpy as np
from contextlib import ExitStack

import concourse.bass as bass
import concourse.mybir as mybir
from concourse.bass_utils import run_bass_kernel_spmd

F32 = mybir.dt.float32
BF16 = mybir.dt.bfloat16
AF = mybir.ActivationFunctionType
ALU = mybir.AluOpType
AX = mybir.AxisListType

D = 1024
DI = 3072
SEQ = 4096
TOK = 2048
H = 16
HD = 64
NHP = 8
MAGIC = float(1.5 * 2 ** 23)
CEXP = 12.0
EPS = 1e-5
QMAX = 32767.0
KB = 1024

ENGINES = ("sync", "vector", "scalar", "tensor", "gpsimd")


class Sem:
    __slots__ = ("h", "n")

    def __init__(self, h):
        self.h = h
        self.n = 0

    def inc(self, amt=1):
        self.n += amt
        return self.n


def build(n_cores=8, phase=99):
    import os
    phase = float(os.environ.get("KPHASE", phase))
    nc = bass.Bass("TRN2", target_bir_lowering=False)

    xq_d = nc.dram_tensor("xq", [TOK, D], F32, kind="ExternalInput")
    xk0_d = nc.dram_tensor("xk0", [TOK, D], F32, kind="ExternalInput")
    xk1_d = nc.dram_tensor("xk1", [TOK, D], F32, kind="ExternalInput")
    w_d = {}
    for p in "qkvo":
        w_d["W" + p] = nc.dram_tensor("W" + p, [D, D], F32, kind="ExternalInput")
        w_d["b" + p] = nc.dram_tensor("b" + p, [D], F32, kind="ExternalInput")
        w_d["A" + p] = nc.dram_tensor("A" + p, [8, D], F32, kind="ExternalInput")
        w_d["B" + p] = nc.dram_tensor("B" + p, [D, 8], F32, kind="ExternalInput")
    rez_d = nc.dram_tensor("rez", [1], F32, kind="ExternalInput")
    wg_d = nc.dram_tensor("w_gate", [DI, D], F32, kind="ExternalInput")
    wu_d = nc.dram_tensor("w_up", [DI, D], F32, kind="ExternalInput")
    wd_d = nc.dram_tensor("w_down", [D, DI], F32, kind="ExternalInput")
    y_d = nc.dram_tensor("y", [TOK, D], F32, kind="ExternalOutput")

    xn32_d = nc.dram_tensor("xn32_s", [128, 8 * TOK], F32)
    xhat_d = nc.dram_tensor("xhat_s", [128, 8 * TOK], BF16)
    wupT_d = nc.dram_tensor("wupT_s", [128, 8 * DI], BF16)
    wgateT_d = nc.dram_tensor("wgateT_s", [128, 8 * DI], BF16)
    wdownT_d = nc.dram_tensor("wdownT_s", [128, 24 * D], BF16)
    woT_d = nc.dram_tensor("woT_s", [128, 8 * D], BF16)
    wqT_d = nc.dram_tensor("wqT_s", [128, 8 * D], BF16)
    wkT_d = nc.dram_tensor("wkT_s", [128, 8 * D], BF16)
    wvT_d = nc.dram_tensor("wvT_s", [128, 8 * D], BF16)
    acat_d = nc.dram_tensor("acat_s", [128, NHP * TOK], BF16)
    upraw_d = nc.dram_tensor("upraw_s", [128, 24 * TOK], F32)
    gateraw_d = nc.dram_tensor("gateraw_s", [128, 24 * TOK], F32)
    m_d = nc.dram_tensor("m_s", [128, 24 * TOK], F32)
    oraw_d = nc.dram_tensor("oraw_s", [128, 24 * TOK], F32)
    cc_in = [nc.dram_tensor(f"ccin{i}", [1, 8], F32) for i in range(4)]
    cc_out = [nc.dram_tensor(f"ccout{i}", [1, 8], F32) for i in range(4)]

    SB_SZ = nc.SBUF_PARTITION_SIZE_BYTES
    base = (int(nc.sbuf_base) + 2048 + 31) & ~31
    TOPB = SB_SZ - 256
    placed = []

    def at(name, shape, dtype, off_kb, off_b=0):
        off = base + int(off_kb * KB) + off_b
        assert off % 32 == 0, name
        nbytes = int(np.prod(shape[1:])) * mybir.dt.size(dtype)
        assert off + nbytes <= TOPB, f"{name} overflow: {off}+{nbytes} vs {TOPB}"
        placed.append((name, off, off + nbytes))
        return nc.alloc_sbuf_tensor_at(name, list(shape), dtype, offset=off)

    ident = at("ident", [128, 128], F32, 0.0)
    ones_row = at("ones_row", [1, 128], F32, 0.5)
    ktab = at("ktab", [128, 32], F32, 1.0)
    biastab = at("biastab", [128, 512], F32, 1.25)
    bcol = {p: at("bcol_" + p, [128, 8], F32, 3.25, i * 32)
            for i, p in enumerate(["q8", "k", "v", "o"])}
    _sn = [0]

    def scal(name):
        t = at("sc_" + name, [128, 1], F32, 3.5, _sn[0] * 32)
        _sn[0] += 1
        assert _sn[0] <= 48
        return t

    s_ = {n: scal(n) for n in [
        "run1", "sum", "sumsq", "mean", "var", "t1", "t2", "t3", "rstd", "negmr",
        "s1", "inv1", "s1b", "inv1b",
        "run_up", "run_gt", "run_m", "run_o",
        "s2b", "inv2b", "s3b", "inv3b", "s4b", "inv4b", "s5b", "inv5b", "magb", "nmagb",
        "rz", "rzb"]}
    cc_sb = at("cc_sb", [1, 8], F32, 5.0)
    cc_sb2 = at("cc_sb2", [1, 8], F32, 5.0, 32)
    bv_row = at("bv_row", [1, D], F32, 5.5)

    ksb = at("ksb", [128, 8 * SEQ], BF16, 10, -128)
    vhat = at("vhat", [128, 32 * 16 * HD], BF16, 74)
    vones = at("vones", [128, HD], BF16, 9.5)
    qsb = at("qsb", [128, 8 * TOK], BF16, 139)
    pbuf = [at(f"pbuf{i}", [128, 1024], BF16, 171, i * 2048) for i in range(4)]
    acat = [at(f"acat{i}", [128, 512], BF16, 179, i * 1024) for i in range(2)]
    recipb = [at("recipb0", [128, 512], F32, 181), at("recipb1", [128, 512], F32, 183)]

    wstage = at("wstage", [128, 8192], F32, 10)
    wstage2 = at("wstage2", [128, 8192], F32, 102)
    wout = at("wout", [128, 24576], BF16, 42)
    bt32 = at("bt32", [8, 1024], F32, 90)
    ab16 = at("ab16", [8, 1024], BF16, 94)
    btb16 = at("btb16", [8, 1024], BF16, 96)
    wscr = at("wscr", [128, 1024], F32, 98)
    wq2 = at("wq2", [128, 8192], BF16, 10)
    wk2 = at("wk2", [128, 8192], BF16, 188)
    wv2 = at("wv2", [128, 8192], BF16, 188)
    xnq32 = at("xnq32", [128, 8 * TOK], F32, 74)
    xhat_sb = at("xhat_sb", [128, 8 * TOK], BF16, 10)
    uscr = at("uscr", [128, TOK], F32, 42)
    xnbq = at("xnbq", [128, 8 * TOK], BF16, 42)
    xnbk = at("xnbk", [128, 32 * 1024], BF16, 74)
    xtok = [at(f"xtok{i}", [128, D], F32, 170, i * 4096) for i in range(2)]
    sqscr = at("sqscr", [128, D], F32, 178)
    xnt = [at(f"xnt{i}", [128, D], F32, 182, i * 4096) for i in range(2)]

    fst = [at(f"fst{i}", [128, TOK], F32, 10, i * 8192) for i in range(8)]
    f1st = [at(f"f1st{i}", [128, TOK], F32, 42, i * 8192) for i in range(2)]
    wupT_sb = at("wupT_sb", [128, 8 * DI], BF16, 139)
    f2g = [at(f"f2g{i}", [128, 512], F32, 185, i * 2048) for i in range(4)]
    f2a = [at(f"f2a{i}", [128, 512], F32, 193, i * 2048) for i in range(4)]
    f2b = [at(f"f2b{i}", [128, 512], F32, 201, i * 2048) for i in range(2)]
    bvb = at("bvb", [128, 1024], F32, 181)
    ohat_sb = at("ohat_sb", [128, 24 * 1024], BF16, 10)
    wdT_sb = at("wdT_sb", [128, 24 * 1024], BF16, 58)
    ghalf = at("ghalf", [128, 8 * 1024], F32, 106)
    woT_sb = at("woT_sb", [128, 8 * D], BF16, 138)
    acat_q = at("acat_q", [128, 8 * 512], BF16, 154)
    xnq_q = at("xnq_q", [128, 8 * 512], F32, 162)
    ot_scr = at("ot_scr", [128, 1024], F32, 178)
    sig_scr = at("sig_scr", [128, 1024], F32, 182)
    wgq = at("wgq", [128, 4096], F32, 185.5)
    xq_tok = at("xq_tok", [128, D], F32, 201.5)
    out_sb = at("out_sb", [128, D], F32, 178)

    hi = max(e_ for _, _, e_ in placed)
    assert hi <= TOPB, f"layout overflow {hi} > {TOPB}"

    steps = []

    with ExitStack() as es:
        e = es.enter_context
        psA = e(nc.psum_tensor("psA", [128, 1024], F32))
        psB = e(nc.psum_tensor("psB", [128, 1024], F32))
        psC = e(nc.psum_tensor("psC", [128, 1024], F32))
        psD = e(nc.psum_tensor("psD", [128, 1024], F32))

        sem = {n: Sem(e(nc.semaphore("sm_" + n))) for n in [
            "lds", "spl", "gio", "gd", "vd", "sd", "td", "cc", "sc", "ex", "av", "st", "bar",
            "bcx", "ac0", "ac1", "fg0", "fg1", "fg2", "fg3",
            "lsm", "lw", "lw2", "ldx0", "ldx1", "lg0", "lg1", "lm0", "lm1", "lo0", "lo1",
            "lq", "lx", "sp0", "sp1", "so0", "so1"]}

        last_kind = {}
        last_op = {}
        tagged = set()

        def A(eng, fn):
            last_kind[eng] = "op"
            last_op[eng] = len(steps)
            steps.append((eng, fn))

        def _done(eng, s):
            i = last_op.get(eng)
            if i is None or i in tagged:
                A(eng, lambda E, h=s.h: E.drain().then_inc(h, 1))
                return s.inc()
            old = steps[i][1]
            steps[i] = (eng, lambda E, old=old, h=s.h: old(E).then_inc(h, 1))
            tagged.add(i)
            return s.inc()

        def W(eng, s, val=None):
            if last_kind.get(eng) == "wait":
                steps.append((eng, lambda E: E.drain()))
            last_kind[eng] = "wait"
            v = s.n if val is None else val
            steps.append((eng, lambda E, h=s.h, v=v: E.wait_ge(h, v)))

        def dma_in(src_ap, dst_ap, eng="sync", sm=None):
            s = sm if sm is not None else (sem["lds"] if eng == "sync" else sem["gio"])
            A(eng, lambda E, o=dst_ap, i=src_ap, h=s.h: E.dma_start(out=o, in_=i).then_inc(h, 16))
            return s.inc(16)

        def dma_spill(dst_ap, src_ap, sm=None):
            s = sm if sm is not None else sem["spl"]
            A("gpsimd", lambda E, o=dst_ap, i=src_ap, h=s.h: E.dma_start(out=o, in_=i).then_inc(h, 16))
            return s.inc(16)

        def v_op(fn):
            A("vector", fn)

        def v_done():
            A("vector", lambda E, h=sem["vd"].h: E.drain().then_inc(h, 1))
            return sem["vd"].inc()

        def v_done_f():
            return _done("vector", sem["vd"])

        def s_done():
            return _done("scalar", sem["sd"])

        def t_done():
            return _done("tensor", sem["td"])

        def vdrain():
            A("vector", lambda E: E.drain())

        _barn = [0]

        def barrier():
            _barn[0] += 1
            for eng in ("vector", "scalar", "tensor", "gpsimd"):
                A(eng, lambda E, h=sem["bar"].h: E.drain().then_inc(h, 1))
            A("sync", lambda E, h=sem["bar"].h: E.sem_inc(h, 1))
            tgt = sem["bar"].inc(5)
            for eng in ENGINES:
                W(eng, sem["bar"], tgt)

        A("gpsimd", lambda E: E.iota(ident[:, :], [[1, 128]], channel_multiplier=0,
                                     allow_small_or_imprecise_dtypes=True))
        A("gpsimd", lambda E: E.iota(sqscr[:, 0:128], [[0, 128]], channel_multiplier=1,
                                     allow_small_or_imprecise_dtypes=True))
        A("gpsimd", lambda E, h=sem["gd"].h: E.iota(ktab[:, :], [[128, 32]], channel_multiplier=1,
                                                    allow_small_or_imprecise_dtypes=True).then_inc(h, 1))
        g_iota = sem["gd"].inc()
        dma_in(rez_d.ap().rearrange("(a b) -> a b", a=1), s_["rz"][0:1, 0:1], sm=sem["lsm"])
        for p, dst in [("q", "q8"), ("k", "k"), ("v", "v"), ("o", "o")]:
            for i in range(8):
                src = w_d["b" + p].ap().rearrange("(a b) -> b a", a=8)[:, i:i + 1]
                dma_in(src, bcol[dst][:, i:i + 1], sm=sem["lsm"])
        dma_in(w_d["bv"].ap().rearrange("(a b) -> a b", a=1), bv_row[0:1, :], sm=sem["lsm"])
        l_small = sem["lsm"].n

        W("vector", sem["gd"], g_iota)
        v_op(lambda E: E.tensor_tensor(ident[:, :], ident[:, :], sqscr[:, 0:128], ALU.is_equal))
        vdrain()
        for h in range(H):
            sl = float(2.0 ** (-h))
            v_op(lambda E, h=h, sl=sl: E.tensor_scalar(
                biastab[:, h * 32:(h + 1) * 32], ktab[:, :], sl, -(sl * 4095.0 + CEXP),
                ALU.mult, ALU.add))
        v_op(lambda E: E.memset(ones_row[0:1, :], 1.0))
        v_op(lambda E: E.memset(vones[:, :], 1.0))
        v_op(lambda E: E.memset(s_["run1"][:, :], 0.0))
        v_op(lambda E: E.memset(s_["run_up"][:, :], 0.0))
        v_op(lambda E: E.memset(s_["run_gt"][:, :], 0.0))
        v_op(lambda E: E.memset(s_["run_m"][:, :], 0.0))
        v_op(lambda E: E.memset(s_["run_o"][:, :], 0.0))
        v_op(lambda E: E.memset(s_["magb"][:, :], MAGIC))
        v_op(lambda E: E.memset(s_["nmagb"][:, :], -MAGIC))
        W("vector", sem["lsm"], l_small)
        v_op(lambda E: E.tensor_scalar(bcol["q8"][:, :], bcol["q8"][:, :], 0.125, None, ALU.mult))
        v_setup = v_done()
        W("tensor", sem["vd"], v_setup)

        def bcast128(srcs, dsts):
            vw = v_done()
            W("tensor", sem["vd"], vw)
            for j, src in enumerate(srcs):
                A("tensor", lambda E, j=j, src=src: E.matmul(
                    psC[:, 512 + j:513 + j], ones_row[0:1, :], src[0:1, 0:1],
                    start=True, stop=True))
            tb = t_done()
            W("vector", sem["td"], tb)
            for j, dst in enumerate(dsts):
                v_op(lambda E, j=j, dst=dst: E.tensor_copy(dst[:, :], psC[:, 512 + j:513 + j]))
            vv = v_done()
            W("tensor", sem["vd"], vv)

        def allreduce(idx):
            A("gpsimd", lambda E, i=idx, h=sem["gio"].h: E.dma_start(
                out=cc_in[i].ap(), in_=cc_sb[0:1, :]).then_inc(h, 16))
            g1 = sem["gio"].inc(16)
            W("gpsimd", sem["gio"], g1)
            if n_cores > 1:
                A("gpsimd", lambda E, i=idx, h=sem["cc"].h: E.collective_compute(
                    "AllReduce", ALU.max, replica_groups=[list(range(n_cores))],
                    ins=[cc_in[i].ap().opt()], outs=[cc_out[i].ap().opt()]).then_inc(h, 1))
                c1 = sem["cc"].inc()
                W("gpsimd", sem["cc"], c1)
            else:
                A("gpsimd", lambda E, i=idx, h=sem["gio"].h: E.dma_start(
                    out=cc_out[i].ap(), in_=cc_in[i].ap()).then_inc(h, 16))
                W("gpsimd", sem["gio"], sem["gio"].inc(16))
            A("gpsimd", lambda E, i=idx, h=sem["gio"].h: E.dma_start(
                out=cc_sb2[0:1, :], in_=cc_out[i].ap()).then_inc(h, 16))
            return sem["gio"].inc(16)

        def xpart_max(src_scal, dst_col):
            vw = v_done()
            W("tensor", sem["vd"], vw)
            A("tensor", lambda E, s=src_scal: E.transpose(psD[0:1, 512:640], s[:, :], ident[:, :]))
            tt = t_done()
            W("vector", sem["td"], tt)
            v_op(lambda E, c=dst_col: E.tensor_reduce(cc_sb[0:1, c:c + 1], psD[0:1, 512:640],
                                                      AX.X, ALU.max))
            vv = v_done()
            W("tensor", sem["vd"], vv)

        wbufs = (wstage, wstage2)
        wbuf_tc = [None, None]
        wrnd = [0]

        def prep_weight(src_d, O, I, out_sbuf, out_dram, lora=None, scale=1.0, lpbq=False):
            oT, cT = O // 128, I // 128
            R = max(1, min(oT, 8192 // I))
            if lora is not None:
                A_d, B_d, lsc = lora
                W("vector", sem["td"], sem["td"].n)
                if wbuf_tc[0] is not None:
                    W("sync", sem["td"], wbuf_tc[0])
                for i in range(8):
                    dma_in(B_d.ap()[i * 128:(i + 1) * 128, :], wstage[:, i * 8:(i + 1) * 8],
                           sm=sem["lw"])
                lb = sem["lw"].n
                W("tensor", sem["lw"], lb)
                for i in range(8):
                    A("tensor", lambda E, i=i: E.transpose(psB[0:8, i * 128:(i + 1) * 128],
                                                           wstage[:, i * 8:(i + 1) * 8], ident[:, :]))
                tb = t_done()
                wbuf_tc[0] = tb
                W("vector", sem["td"], tb)
                v_op(lambda E, lsc=lsc: E.tensor_scalar(btb16[0:8, :], psB[0:8, :], lsc, None, ALU.mult))
                W("gpsimd", sem["td"], sem["td"].n)
                ga = dma_in(A_d.ap(), ab16[0:8, :], eng="gpsimd")
                vb = v_done()
                W("tensor", sem["vd"], vb)
                W("tensor", sem["gio"], ga)
            n_rounds = (oT + R - 1) // R
            lwv = {}

            def issue_loads(i):
                o0_, o1_ = i * R, min(oT, i * R + R)
                b = (wrnd[0] + i) % 2
                buf = wbufs[b]
                lsem = sem["lw"] if b == 0 else sem["lw2"]
                if wbuf_tc[b] is not None:
                    W("sync", sem["td"], wbuf_tc[b])
                for o in range(o0_, o1_):
                    dma_in(src_d.ap()[o * 128:(o + 1) * 128, :],
                           buf[:, (o - o0_) * I:(o - o0_ + 1) * I], sm=lsem)
                lwv[i] = lsem.n

            issue_loads(0)
            for rnd in range(n_rounds):
                o0, o1 = rnd * R, min(oT, rnd * R + R)
                if rnd + 1 < n_rounds:
                    issue_loads(rnd + 1)
                b = (wrnd[0] + rnd) % 2
                wstg = wbufs[b]
                lsem = sem["lw"] if b == 0 else sem["lw2"]
                lw = lwv[rnd]
                W("tensor", lsem, lw)
                if lpbq:
                    span = (o1 - o0) * I
                    nb = span // 32
                    W("vector", lsem, lw)
                    v_op(lambda E, nb=nb, wstg=wstg: E.tensor_reduce(
                        wscr[:, 0:nb],
                        bass.AP(wstg, 0, [[wstg.shape[1], 128], [32, nb], [1, 32]]),
                        AX.X, ALU.max, apply_absolute_value=True))
                    vdrain()
                    v_op(lambda E, nb=nb: E.tensor_scalar(
                        wscr[:, nb:2 * nb], wscr[:, 0:nb], 1.0 / 7.0, 1e-12,
                        ALU.mult, ALU.max))
                    vdrain()
                    v_op(lambda E, nb=nb: E.reciprocal(wscr[:, 2 * nb:3 * nb],
                                                       wscr[:, nb:2 * nb]))
                    vdrain()
                    v_op(lambda E, nb=nb, span=span, wstg=wstg: E.tensor_tensor(
                        wstg[:, 0:span], wstg[:, 0:span],
                        bass.AP(wscr, 2 * nb, [[wscr.shape[1], 128], [1, nb], [0, 32]]),
                        ALU.mult))
                    v_mul = v_done_f()
                    W("scalar", sem["vd"], v_mul)
                    A("scalar", lambda E, span=span, wstg=wstg: E.activation(
                        wstg[:, 0:span], wstg[:, 0:span], AF.Identity,
                        bias=s_["magb"][:, :], scale=1.0))
                    s_mg = s_done()
                    W("vector", sem["sd"], s_mg)
                    v_op(lambda E, nb=nb, span=span, wstg=wstg: E.scalar_tensor_tensor(
                        wstg[:, 0:span], wstg[:, 0:span], MAGIC,
                        bass.AP(wscr, nb, [[wscr.shape[1], 128], [1, nb], [0, 32]]),
                        ALU.subtract, ALU.mult))
                    v_lp = v_done_f()
                    W("tensor", sem["vd"], v_lp)
                last_tc = None
                for c in range(cT):
                    ps = psA if (c % 2 == 0) else psB
                    if lora is not None:
                        for j in range(2):
                            A("tensor", lambda E, c=c, j=j: E.matmul(
                                psC[:, j * 512:(j + 1) * 512], ab16[0:8, c * 128:(c + 1) * 128],
                                btb16[0:8, j * 512:(j + 1) * 512], start=True, stop=True))
                    for o in range(o0, o1):
                        A("tensor", lambda E, c=c, o=o, o0=o0, ps=ps, I=I, wstg=wstg: E.transpose(
                            ps[:, (o - o0) * 128:(o - o0 + 1) * 128],
                            wstg[:, (o - o0) * I + c * 128:(o - o0) * I + (c + 1) * 128],
                            ident[:, :]))
                    tc = t_done()
                    last_tc = tc
                    ncols = (o1 - o0) * 128
                    if lora is not None:
                        W("scalar", sem["td"], tc)
                        A("scalar", lambda E, ps=ps, n=ncols: E.copy(wscr[:, 0:n], ps[:, 0:n]))
                        sc_t = s_done()
                        W("vector", sem["sd"], sc_t)
                        v_op(lambda E, c=c, n=ncols, sc=scale, O=O: E.scalar_tensor_tensor(
                            out_sbuf[:, c * O: c * O + n], wscr[:, 0:n], sc, psC[:, 0:n],
                            ALU.mult, ALU.add))
                    elif c % 2 == 0:
                        W("vector", sem["td"], tc)
                        v_op(lambda E, c=c, o0=o0, n=ncols, ps=ps, O=O: E.tensor_copy(
                            out_sbuf[:, c * O + o0 * 128: c * O + o0 * 128 + n], ps[:, 0:n]))
                    else:
                        W("scalar", sem["td"], tc)
                        A("scalar", lambda E, c=c, o0=o0, n=ncols, ps=ps, O=O: E.copy(
                            out_sbuf[:, c * O + o0 * 128: c * O + o0 * 128 + n], ps[:, 0:n]))
                    if lora is not None or c % 2 == 0:
                        vc = v_done_f()
                        W("tensor", sem["vd"], vc)
                    else:
                        sc2 = s_done()
                        W("tensor", sem["sd"], sc2)
                wbuf_tc[b] = last_tc
            wrnd[0] += n_rounds
            vs = v_done_f()
            if out_dram is not None:
                W("gpsimd", sem["vd"], vs)
                W("gpsimd", sem["sd"], sem["sd"].n)
                for c in range(cT):
                    dma_spill(out_dram.ap()[:, c * O:(c + 1) * O],
                              out_sbuf[:, c * O:(c + 1) * O])
                sp = sem["spl"].n
                W("vector", sem["spl"], sp)
                W("scalar", sem["spl"], sp)

        prep_weight(w_d["Wq"], D, D, wout, wqT_d, lora=(w_d["Aq"], w_d["Bq"], 1.0 / 64.0), scale=0.125)
        prep_weight(w_d["Wk"], D, D, wout, wkT_d, lora=(w_d["Ak"], w_d["Bk"], 0.125))
        prep_weight(w_d["Wv"], D, D, wout, wvT_d, lora=(w_d["Av"], w_d["Bv"], 0.125))
        prep_weight(w_d["Wo"], D, D, wout, woT_d, lora=(w_d["Ao"], w_d["Bo"], 0.125))
        prep_weight(wu_d, DI, D, wout, wupT_d, lpbq=True)
        prep_weight(wg_d, DI, D, wout, wgateT_d, lpbq=True)
        prep_weight(wd_d, D, DI, wout, wdownT_d, lpbq=True)
        barrier()

        if phase < 2:
            return _finish(nc, steps, sem, A, W, e)
        def ln_tiles(src_d, nt, dst32, dstbf, track_max):
            norm_done = []
            tr_done = []
            evac_done = []
            W("sync", sem["sd"], sem["sd"].n)
            for i in range(nt):
                xt = xtok[i % 2]
                ldx = sem["ldx0"] if i % 2 == 0 else sem["ldx1"]
                if i >= 2:
                    A("sync", lambda E, h=sem["sd"].h, v=norm_done[i - 2]: E.wait_ge(h, v))
                lv = dma_in(src_d.ap()[i * 128:(i + 1) * 128, :], xt[:, :], sm=ldx)
                W("scalar", ldx, lv)
                W("vector", ldx, lv)
                v_op(lambda E, xt=xt: E.bn_stats(sqscr[:, 0:6], xt[:, 0:512]))
                v_op(lambda E, xt=xt: E.bn_stats(sqscr[:, 6:12], xt[:, 512:1024]))
                vdrain()
                v_op(lambda E: E.bn_aggr(sqscr[:, 12:14], sqscr[:, 0:12]))
                vdrain()
                v_op(lambda E: E.tensor_scalar(s_["t2"][:, :], sqscr[:, 13:14], 1.0, EPS,
                                               ALU.mult, ALU.add))
                v_var = v_done()
                W("scalar", sem["vd"], v_var)
                A("scalar", lambda E: E.activation(s_["t2"][:, :], s_["t2"][:, :], AF.Sqrt,
                                                   bias=0.0, scale=1.0))
                s_rs = s_done()
                W("vector", sem["sd"], s_rs)
                v_op(lambda E: E.reciprocal(s_["rstd"][:, :], s_["t2"][:, :]))
                vdrain()
                v_op(lambda E: E.tensor_tensor(s_["negmr"][:, :], sqscr[:, 12:13], s_["rstd"][:, :], ALU.mult))
                vdrain()
                v_op(lambda E: E.tensor_scalar(s_["negmr"][:, :], s_["negmr"][:, :], -1.0, None, ALU.mult))
                v_stats = v_done()
                xo = xnt[i % 2]
                W("scalar", sem["vd"], v_stats)
                if i >= 2:
                    A("scalar", lambda E, h=sem["td"].h, v=tr_done[i - 2]: E.wait_ge(h, v))
                A("scalar", lambda E, xt=xt, xo=xo: E.activation(xo[:, :], xt[:, :], AF.Identity,
                                                                bias=s_["negmr"][:, :],
                                                                scale=s_["rstd"][:, :]))
                s_norm = s_done()
                norm_done.append(s_norm)
                if track_max is not None:
                    W("vector", sem["sd"], s_norm)
                    v_op(lambda E, xo=xo: E.tensor_reduce(s_["t1"][:, :], xo[:, :], AX.X, ALU.max,
                                                          apply_absolute_value=True))
                    vdrain()
                    v_op(lambda E, tm=track_max: E.tensor_tensor(tm[:, :], tm[:, :], s_["t1"][:, :], ALU.max))
                    vdrain()
                ps = psA if (i % 2 == 0) else psB
                W("tensor", sem["sd"], s_norm)
                if i >= 2:
                    A("tensor", lambda E, h=sem["vd"].h, v=evac_done[i - 2][0]: E.wait_ge(h, v))
                    A("tensor", lambda E, h=sem["sd"].h, v=evac_done[i - 2][1]: E.wait_ge(h, v))
                for f in range(8):
                    A("tensor", lambda E, f=f, ps=ps, xo=xo: E.transpose(
                        ps[:, f * 128:(f + 1) * 128], xo[:, f * 128:(f + 1) * 128], ident[:, :]))
                t_tr = t_done()
                tr_done.append(t_tr)
                ve, se2 = sem["vd"].n, sem["sd"].n
                if dst32 is not None:
                    dst, stride, toff = dst32
                    W("vector", sem["td"], t_tr)
                    v_op(lambda E, i=i, ps=ps, dst=dst, stride=stride, toff=toff: E.tensor_copy(
                        bass.AP(dst, toff + i * 128, [[dst.shape[1], 128], [stride, 8], [1, 128]]),
                        ps[:, :]))
                    ve = v_done()
                if dstbf is not None:
                    dst, offfn = dstbf
                    co, fs = offfn(i)
                    W("scalar", sem["td"], t_tr)
                    A("scalar", lambda E, ps=ps, dst=dst, co=co, fs=fs: E.activation(
                        bass.AP(dst, co, [[dst.shape[1], 128], [fs, 8], [1, 128]]),
                        ps[:, :], AF.Identity, bias=0.0, scale=1.0))
                    se2 = s_done()
                evac_done.append((ve, se2))
            W("tensor", sem["vd"], sem["vd"].n)
            W("tensor", sem["sd"], sem["sd"].n)

        ln_tiles(xq_d, 16, (xnq32, TOK, 0), None, s_["run1"])

        v_op(lambda E: E.memset(cc_sb[0:1, :], 0.0))
        vdrain()
        xpart_max(s_["run1"], 0)
        v_cc = v_done()
        W("gpsimd", sem["vd"], v_cc)
        g_ar = allreduce(0)
        W("vector", sem["gio"], g_ar)
        v_op(lambda E: E.tensor_scalar(s_["s1"][0:1, :], cc_sb2[0:1, 0:1], 1.0 / QMAX, 1e-12,
                                       ALU.mult, ALU.max))
        vdrain()
        v_op(lambda E: E.reciprocal(s_["inv1"][0:1, :], s_["s1"][0:1, :]))
        bcast128([s_["s1"], s_["inv1"], s_["rz"]], [s_["s1b"], s_["inv1b"], s_["rzb"]])

        for f in range(8):
            v_op(lambda E, f=f: E.tensor_scalar(uscr[:, :], xnq32[:, f * TOK:(f + 1) * TOK],
                                                s_["inv1b"][:, :], MAGIC, ALU.mult, ALU.add))
            vdrain()
            v_op(lambda E, f=f: E.tensor_scalar(xhat_sb[:, f * TOK:(f + 1) * TOK], uscr[:, :],
                                                1.0, -MAGIC, ALU.mult, ALU.add))
            vdrain()
        v_xh = v_done()
        W("gpsimd", sem["vd"], v_xh)
        dma_spill(xn32_d.ap(), xnq32[:, :])
        barrier()

        if phase < 3:
            return _finish(nc, steps, sem, A, W, e)
        W("scalar", sem["spl"], sem["spl"].n)
        ln_tiles(xk0_d, 16, None, (xnbk, lambda i: (i * 1024, 128)), None)
        ln_tiles(xk1_d, 16, None, (xnbk, lambda i: ((i + 16) * 1024, 128)), None)
        barrier()


        if phase < 3.5:
            return _finish(nc, steps, sem, A, W, e)
        W("gpsimd", sem["spl"], sem["spl"].n)
        W("sync", sem["spl"], sem["spl"].n)
        for c in range(8):
            dma_in(wupT_d.ap()[:, c * DI:(c + 1) * DI], wupT_sb[:, c * DI:(c + 1) * DI])
        l_w1 = sem["lds"].n
        W("tensor", sem["lds"], l_w1)

        def f1_weight(wT, out_dram, runmax, nit=24):
            A("scalar", lambda E, h=sem["sp0"].h, v=sem["sp0"].n: E.wait_ge(h, v))
            A("scalar", lambda E, h=sem["sp1"].h, v=sem["sp1"].n: E.wait_ge(h, v))
            stg_done = []
            grp_done = []
            for it in range(nit):
                stg = f1st[it % 2]
                sp = sem["sp0"] if it % 2 == 0 else sem["sp1"]
                if it >= 2:
                    A("scalar", lambda E, h=sp.h, v=stg_done[it - 2]: E.wait_ge(h, v))
                for t0 in range(0, TOK, 512):
                    ps = psA if ((t0 // 512) % 2 == 0) else psB
                    if len(grp_done) >= 2:
                        sfp, vvp = grp_done[-2]
                        W("tensor", sem["sd"], sfp)
                        W("tensor", sem["vd"], vvp)
                    for c in range(8):
                        A("tensor", lambda E, it=it, c=c, t0=t0, ps=ps, wT=wT: E.matmul(
                            ps[:, 0:512], wT[:, c * DI + it * 128: c * DI + (it + 1) * 128],
                            xhat_sb[:, c * TOK + t0: c * TOK + t0 + 512],
                            start=(c == 0), stop=(c == 7)))
                    tf = t_done()
                    W("scalar", sem["td"], tf)
                    A("scalar", lambda E, t0=t0, ps=ps, stg=stg: E.activation(
                        stg[:, t0:t0 + 512], ps[:, 0:512], AF.Identity,
                        bias=0.0, scale=s_["s1b"][:, :]))
                    sf = s_done()
                    W("vector", sem["sd"], sf)
                    tsl = s_["t1"] if (t0 // 512) % 2 == 0 else s_["t2"]
                    v_op(lambda E, ps=ps, tsl=tsl: E.tensor_reduce(tsl[:, :], ps[:, 0:512], AX.X,
                                                                   ALU.max, apply_absolute_value=True))
                    vdrain()
                    v_op(lambda E, rm=runmax, tsl=tsl: E.tensor_tensor(rm[:, :], rm[:, :], tsl[:, :], ALU.max))
                    vv = v_done_f()
                    grp_done.append((sf, vv))
                s_stg = sem["sd"].n
                W("gpsimd", sem["sd"], s_stg)
                dma_spill(out_dram.ap()[:, it * TOK:(it + 1) * TOK], stg[:, :], sm=sp)
                stg_done.append(sp.n)
            W("tensor", sem["sd"], sem["sd"].n)
            W("tensor", sem["vd"], sem["vd"].n)

        f1_weight(wupT_sb, upraw_d, s_["run_up"])
        W("sync", sem["td"], sem["td"].n)
        for c in range(8):
            dma_in(wgateT_d.ap()[:, c * DI:(c + 1) * DI], wupT_sb[:, c * DI:(c + 1) * DI])
        W("tensor", sem["lds"], sem["lds"].n)
        f1_weight(wupT_sb, gateraw_d, s_["run_gt"])

        v_op(lambda E: E.memset(cc_sb[0:1, :], 0.0))
        vdrain()
        xpart_max(s_["run_up"], 0)
        xpart_max(s_["run_gt"], 1)
        v_cc = v_done()
        W("gpsimd", sem["vd"], v_cc)
        g_ar = allreduce(1)
        W("vector", sem["gio"], g_ar)
        for col, sn, invn in ((0, "s2b", "inv2b"), (1, "s3b", "inv3b")):
            v_op(lambda E, col=col: E.tensor_scalar(s_["t1"][0:1, 0:1], cc_sb2[0:1, col:col + 1],
                                                    s_["s1"][0:1, 0:1], None, ALU.mult))
            vdrain()
            v_op(lambda E: E.tensor_scalar(s_["t1"][0:1, 0:1], s_["t1"][0:1, 0:1], 1.0 / QMAX,
                                           1e-12, ALU.mult, ALU.max))
            vdrain()
            v_op(lambda E: E.reciprocal(s_["t3"][0:1, 0:1], s_["t1"][0:1, 0:1]))
            bcast128([s_["t1"], s_["t3"]], [s_[sn], s_[invn]])
        barrier()

        if phase < 4:
            return _finish(nc, steps, sem, A, W, e)
        W("gpsimd", sem["spl"], sem["spl"].n)
        W("gpsimd", sem["sp0"], sem["sp0"].n)
        W("gpsimd", sem["sp1"], sem["sp1"].n)
        A("gpsimd", lambda E, h=sem["gio"].h: E.dma_start(out=xnbq[:, :], in_=xn32_d.ap()).then_inc(h, 16))
        g_xnbq = sem["gio"].inc(16)

        W("sync", sem["spl"], sem["spl"].n)
        W("sync", sem["sp0"], sem["sp0"].n)
        W("sync", sem["sp1"], sem["sp1"].n)
        l_wq = dma_in(wqT_d.ap(), wq2[:, :])
        W("tensor", sem["lds"], l_wq)
        W("tensor", sem["gio"], g_xnbq)
        for o in range(8):
            for t0 in range(0, TOK, 512):
                ps = psA if (((o * 4) + t0 // 512) % 2 == 0) else psB
                for c in range(8):
                    A("tensor", lambda E, o=o, c=c, t0=t0, ps=ps: E.matmul(
                        ps[:, 0:512], wq2[:, c * D + o * 128: c * D + (o + 1) * 128],
                        xnbq[:, c * TOK + t0: c * TOK + t0 + 512],
                        start=(c == 0), stop=(c == 7)))
                tp = t_done()
                W("scalar", sem["td"], tp)
                A("scalar", lambda E, o=o, t0=t0, ps=ps: E.activation(
                    qsb[:, o * TOK + t0: o * TOK + t0 + 512], ps[:, 0:512], AF.Identity,
                    bias=bcol["q8"][:, o:o + 1], scale=1.0))
                se2 = s_done()
                W("tensor", sem["sd"], se2)

        W("sync", sem["td"], sem["td"].n)
        l_wk = dma_in(wkT_d.ap(), wk2[:, :])
        W("tensor", sem["lds"], l_wk)
        for o in range(8):
            for t0 in range(0, SEQ, 512):
                ps = psA if (((o * 8) + t0 // 512) % 2 == 0) else psB
                for c in range(8):
                    A("tensor", lambda E, o=o, c=c, t0=t0, ps=ps: E.matmul(
                        ps[:, 0:512],
                        wk2[:, c * D + o * 128: c * D + (o + 1) * 128],
                        bass.AP(xnbk, (t0 // 128) * 1024 + c * 128,
                                [[xnbk.shape[1], 128], [1024, 4], [1, 128]]),
                        start=(c == 0), stop=(c == 7)))
                tp = t_done()
                W("scalar", sem["td"], tp)
                A("scalar", lambda E, o=o, t0=t0, ps=ps: E.activation(
                    ksb[:, o * SEQ + t0: o * SEQ + t0 + 512], ps[:, 0:512], AF.Identity,
                    bias=bcol["k"][:, o:o + 1], scale=1.0))
                se2 = s_done()
                W("tensor", sem["sd"], se2)

        vw = v_done()
        W("tensor", sem["vd"], vw)
        for j in range(2):
            A("tensor", lambda E, j=j: E.matmul(psC[:, j * 512:(j + 1) * 512], ones_row[0:1, :],
                                                bv_row[0:1, j * 512:(j + 1) * 512],
                                                start=True, stop=True))
        t_bv = t_done()
        W("vector", sem["td"], t_bv)
        v_op(lambda E: E.tensor_copy(bvb[:, :], psC[:, :]))
        v_bv = v_done()
        W("tensor", sem["vd"], v_bv)

        W("sync", sem["td"], sem["td"].n)
        l_wv = dma_in(wvT_d.ap(), wv2[:, :])
        W("tensor", sem["lds"], l_wv)
        for tt in range(32):
            ps = psA if (tt % 2 == 0) else psB
            for c in range(8):
                for j in range(2):
                    A("tensor", lambda E, tt=tt, c=c, j=j, ps=ps: E.matmul(
                        ps[:, j * 512:(j + 1) * 512],
                        xnbk[:, tt * 1024 + c * 128: tt * 1024 + (c + 1) * 128],
                        wv2[:, c * D + j * 512: c * D + (j + 1) * 512],
                        start=(c == 0), stop=(c == 7)))
            tv = t_done()
            W("vector", sem["td"], tv)
            v_op(lambda E, tt=tt, ps=ps: E.scalar_tensor_tensor(
                bass.AP(vhat, tt * 1024, [[vhat.shape[1], 128], [HD, 16], [1, HD]]),
                ps[:, :], 1.0, bvb[:, :], ALU.mult, ALU.add))
            vv = v_done_f()
            W("tensor", sem["vd"], vv)
        barrier()

        if phase < 5:
            return _finish(nc, steps, sem, A, W, e)
        skipB = os.environ.get("KSKIPB", "") == "1"
        first = True
        tt_hist = {}
        bc_hist = {}
        spill_hist = {}
        rc_hist = {}

        NF2 = 96
        f2_loads = {}
        f2_vg = {}
        f2_sig = {}
        f2_vm = {}
        f2_vm2 = {}
        f2_spill = {}
        f2_p1 = [0]
        f2_p2 = [0]

        def f2_col(ch):
            it_, qt = divmod(ch, 4)
            return it_ * TOK + qt * 512

        def f2_lg(ch):
            return sem["fg%d" % (ch % 4)]

        def f2_so(ch):
            return sem[["so0", "so1", "sp0", "sp1"][ch % 4]]

        def f2_issue_load(ch):
            if ch >= NF2 or skipB or ch in f2_loads:
                return
            lg = f2_lg(ch)
            so = f2_so(ch)
            if ch >= 4:
                A("gpsimd", lambda E, h=so.h, v=f2_spill[ch - 4]: E.wait_ge(h, v))
            f2_loads[ch] = dma_in(
                gateraw_d.ap()[:, f2_col(ch): f2_col(ch) + 512],
                f2g[ch % 4][:, :], eng="gpsimd", sm=lg)

        def f2_phase1(ch):
            gin, scrA = f2g[ch % 4], f2a[ch % 4]
            W("vector", f2_lg(ch), f2_loads[ch])
            if ch >= 4:
                W("vector", sem["sd"], f2_sig[ch - 4])
            v_op(lambda E, gin=gin: E.tensor_scalar(gin[:, :], gin[:, :], s_["inv3b"][:, :],
                                                    MAGIC, ALU.mult, ALU.add))
            vdrain()
            v_op(lambda E, gin=gin: E.tensor_scalar(gin[:, :], gin[:, :], 1.0, -MAGIC,
                                                    ALU.mult, ALU.add))
            vdrain()
            v_op(lambda E, gin=gin, scrA=scrA: E.tensor_scalar(scrA[:, :], gin[:, :],
                                                               s_["s3b"][:, :], None, ALU.mult))
            f2_vg[ch] = v_done_f()

        def f2_phase2(ch):
            gin, scrA, scrB = f2g[ch % 4], f2a[ch % 4], f2b[ch % 2]
            so = f2_so(ch)
            W("scalar", sem["vd"], f2_vg[ch])
            if ch >= 2:
                W("scalar", sem["vd"], f2_vm2.get(ch - 2, 0))
            A("scalar", lambda E, scrA=scrA, scrB=scrB: E.activation(
                scrB[:, :], scrA[:, :], AF.Sigmoid, bias=0.0, scale=1.0))
            f2_sig[ch] = s_done()
            W("vector", sem["sd"], f2_sig[ch])
            v_op(lambda E, scrB=scrB: E.tensor_scalar(scrB[:, :], scrB[:, :], 65536.0, MAGIC,
                                                      ALU.mult, ALU.add))
            vdrain()
            v_op(lambda E, scrB=scrB: E.tensor_scalar(scrB[:, :], scrB[:, :], 1.0, -MAGIC,
                                                      ALU.mult, ALU.add))
            vdrain()
            v_op(lambda E, scrB=scrB: E.tensor_scalar(scrB[:, :], scrB[:, :], float(2.0 ** -16),
                                                      None, ALU.mult))
            vdrain()
            v_op(lambda E, gin=gin, scrA=scrA, scrB=scrB: E.tensor_tensor(
                gin[:, :], scrA[:, :], scrB[:, :], ALU.mult))
            f2_vm2[ch] = v_done_f()
            vdrain()
            v_op(lambda E, gin=gin: E.tensor_reduce(s_["t1"][:, :], gin[:, :], AX.X, ALU.max,
                                                    apply_absolute_value=True))
            vdrain()
            v_op(lambda E: E.tensor_tensor(s_["run_m"][:, :], s_["run_m"][:, :],
                                           s_["t1"][:, :], ALU.max))
            f2_vm[ch] = v_done_f()
            W("gpsimd", sem["vd"], f2_vm[ch])
            dma_spill(m_d.ap()[:, f2_col(ch): f2_col(ch) + 512], gin[:, :], sm=so)
            f2_spill[ch] = so.n
            f2_issue_load(ch + 4)

        if not skipB:
            W("gpsimd", sem["sp0"], sem["sp0"].n)
            W("gpsimd", sem["sp1"], sem["sp1"].n)
            for _c in range(4):
                f2_issue_load(_c)

        def norm_tt(idx):
            half = idx % 2
            ac = acat[idx % 2]
            sp = sem["ac0"] if idx % 2 == 0 else sem["ac1"]
            if idx >= 2:
                W("vector", sp, spill_hist[idx - 2])
            W("vector", sem["bcx"], bc_hist[idx])
            for j, psav in enumerate((psC, psD)):
                v_op(lambda E, j=j, psav=psav, ac=ac, half=half: E.tensor_tensor(
                    ac[j * 64:(j + 1) * 64, 0:512],
                    psav[0:64, half * 512: half * 512 + 512],
                    recipb[j][0:64, 0:512], ALU.mult))
            tt_hist[idx] = v_done_f()
            W("gpsimd", sem["vd"], tt_hist[idx])
            hp_, qb_ = divmod(idx, 4)
            dma_spill(acat_d.ap()[:, hp_ * TOK + qb_ * 512: hp_ * TOK + qb_ * 512 + 512],
                      ac[:, 0:512], sm=sp)
            spill_hist[idx] = sp.n

        KT0 = []
        for hp in range(NHP):
            smin = 2.0 ** (-(2 * hp + 1))
            k0 = 0
            while k0 < 32 and smin * (4095.0 - (k0 * 128 + 127)) > 64.0:
                k0 += 1
            KT0.append(k0)

        for hp in range(NHP if not skipB else 0):
            kt0 = KT0[hp]
            for qb in range(4):
                idx = hp * 4 + qb
                half = idx % 2
                q0 = qb * 512
                exp_hist = []
                for kt in range(kt0, 33):
                    nk = kt - kt0
                    if kt < 32:
                        par = kt % 2
                        if nk >= 2:
                            W("tensor", sem["ex"], exp_hist[nk - 2])
                        elif not first:
                            W("tensor", sem["ex"], sem["ex"].n - 1)
                        for h, ps in ((0, psA), (1, psB)):
                            A("tensor", lambda E, hp=hp, h=h, q0=q0, kt=kt, par=par, ps=ps: E.matmul(
                                ps[:, par * 512:(par + 1) * 512],
                                ksb[h * 64:(h + 1) * 64,
                                    hp * SEQ + kt * 128: hp * SEQ + (kt + 1) * 128],
                                qsb[h * 64:(h + 1) * 64,
                                    hp * TOK + q0: hp * TOK + q0 + 512],
                                start=True, stop=True, tile_position=(h * 64, 0)))
                        _done("tensor", sem["sc"])
                    if kt >= kt0 + 1:
                        ktm = kt - 1
                        parm = ktm % 2
                        pb1, pb2 = pbuf[parm * 2], pbuf[parm * 2 + 1]
                        W("tensor", sem["ex"], sem["ex"].n)
                        if ktm == kt0 and idx >= 2:
                            W("tensor", sem["vd"], tt_hist[idx - 2])
                        for pb, psav, h in ((pb1, psC, 2 * hp), (pb2, psD, 2 * hp + 1)):
                            oo = ktm * 1024 + h * 64
                            A("tensor", lambda E, oo=oo, pb=pb, psav=psav, ktm=ktm, half=half, kt0=kt0: E.matmul(
                                psav[0:64, half * 512: half * 512 + 512],
                                bass.AP(vhat, oo, [[vhat.shape[1], 128], [1, 64]]),
                                pb[:, 0:512],
                                start=(ktm == kt0), stop=(ktm == 31)))
                            A("tensor", lambda E, pb=pb, psav=psav, ktm=ktm, half=half, kt0=kt0: E.matmul(
                                psav[64:128, (1 - half) * 512: (1 - half) * 512 + 512],
                                vones[:, 0:HD], pb[:, 0:512],
                                start=(ktm == kt0), stop=(ktm == 31)))
                        _done("tensor", sem["av"])
                    if kt < 32:
                        first = False
                        W("scalar", sem["sc"], sem["sc"].n)
                        if nk >= 2:
                            W("scalar", sem["av"], sem["av"].n - 1)
                        par = kt % 2
                        pb1, pb2 = pbuf[par * 2], pbuf[par * 2 + 1]
                        A("scalar", lambda E, hp=hp, kt=kt, par=par, pb1=pb1: E.activation(
                            pb1[:, 0:512], psA[:, par * 512:(par + 1) * 512], AF.Exp,
                            bias=biastab[:, (2 * hp) * 32 + kt:(2 * hp) * 32 + kt + 1],
                            scale=1.0))
                        A("scalar", lambda E, hp=hp, kt=kt, par=par, pb2=pb2: E.activation(
                            pb2[:, 0:512], psB[:, par * 512:(par + 1) * 512], AF.Exp,
                            bias=biastab[:, (2 * hp + 1) * 32 + kt:(2 * hp + 1) * 32 + kt + 1],
                            scale=1.0))
                        exp_hist.append(_done("scalar", sem["ex"]))
                W("vector", sem["av"], sem["av"].n)
                if idx >= 1:
                    W("vector", sem["bcx"], bc_hist[idx - 1])
                for j, psav in enumerate((psC, psD)):
                    v_op(lambda E, j=j, psav=psav, half=half: E.reciprocal(
                        recipb[j][64:65, 0:512],
                        psav[64:65, (1 - half) * 512: (1 - half) * 512 + 512]))
                rc_hist[idx] = v_done_f()
                if idx >= 1:
                    norm_tt(idx - 1)
                W("gpsimd", sem["vd"], tt_hist[idx - 1] if idx >= 1 else rc_hist[idx])
                for j in range(2):
                    A("gpsimd", lambda E, j=j, h=sem["bcx"].h: E.dma_start(
                        out=recipb[j][0:64, 0:512],
                        in_=bass.AP(recipb[j], 64 * recipb[j].shape[1],
                                    [[recipb[j].shape[1], 1], [0, 64], [1, 512]])
                    ).then_inc(h, 16))
                    sem["bcx"].inc(16)
                bc_hist[idx] = sem["bcx"].n
                loaded = len(f2_loads)
                while f2_p2[0] < f2_p1[0]:
                    f2_phase2(f2_p2[0])
                    f2_p2[0] += 1
                while f2_p1[0] < min(NF2, loaded, (idx + 1) * 3):
                    f2_phase1(f2_p1[0])
                    f2_p1[0] += 1
        if not skipB:
            norm_tt(NHP * 4 - 1)
            while f2_p1[0] < NF2:
                f2_issue_load(f2_p1[0])
                f2_phase1(f2_p1[0])
                f2_p1[0] += 1
                while f2_p2[0] < max(0, f2_p1[0] - 2):
                    f2_phase2(f2_p2[0])
                    f2_p2[0] += 1
            while f2_p2[0] < NF2:
                f2_phase2(f2_p2[0])
                f2_p2[0] += 1
        W("tensor", sem["vd"], sem["vd"].n)
        for _ps in (psA, psB, psC, psD):
            A("tensor", lambda E, _ps=_ps: E.matmul(_ps[0:64, 0:512], vones[:, 0:64],
                                                    qsb[:, 0:512], start=True, stop=True))
        A("tensor", lambda E: E.drain())
        barrier()

        if phase < 6:
            return _finish(nc, steps, sem, A, W, e)
        v_op(lambda E: E.memset(cc_sb[0:1, :], 0.0))
        vdrain()
        xpart_max(s_["run_m"], 0)
        v_cc = v_done()
        W("gpsimd", sem["vd"], v_cc)
        g_ar = allreduce(2)
        W("vector", sem["gio"], g_ar)
        v_op(lambda E: E.tensor_scalar(s_["t1"][0:1, 0:1], cc_sb2[0:1, 0:1], 1.0 / QMAX, 1e-12,
                                       ALU.mult, ALU.max))
        vdrain()
        v_op(lambda E: E.reciprocal(s_["t3"][0:1, 0:1], s_["t1"][0:1, 0:1]))
        bcast128([s_["t1"], s_["t3"]], [s_["s4b"], s_["inv4b"]])
        barrier()

        if phase < 8:
            return _finish(nc, steps, sem, A, W, e)
        W("sync", sem["sp0"], sem["sp0"].n)
        W("sync", sem["sp1"], sem["sp1"].n)
        W("sync", sem["so0"], sem["so0"].n)
        W("sync", sem["so1"], sem["so1"].n)
        in_done = []
        oo_done = []
        sgd = []
        v_prev = []
        for it in range(24):
            min_ = fst[it % 2]
            uin = fst[2 + it % 2]
            oo = fst[5 + it % 2]
            lm = sem["lm0"] if it % 2 == 0 else sem["lm1"]
            so = sem["so0"] if it % 2 == 0 else sem["so1"]
            if it >= 2:
                A("sync", lambda E, h=sem["vd"].h, v=in_done[it - 2]: E.wait_ge(h, v))
                A("sync", lambda E, h=sem["sd"].h, v=sgd[it - 2]: E.wait_ge(h, v))
            dma_in(m_d.ap()[:, it * TOK:(it + 1) * TOK], min_[:, :], sm=lm)
            lv = dma_in(upraw_d.ap()[:, it * TOK:(it + 1) * TOK], uin[:, :], sm=lm)
            W("scalar", lm, lv)
            if it >= 1:
                W("scalar", sem["vd"], v_prev[it - 1])
            A("scalar", lambda E, min_=min_: E.activation(
                fst[4][:, :], min_[:, :], AF.Identity,
                bias=s_["magb"][:, :], scale=s_["inv4b"][:, :]))
            A("scalar", lambda E: E.drain())
            A("scalar", lambda E: E.activation(
                fst[4][:, :], fst[4][:, :], AF.Identity,
                bias=s_["nmagb"][:, :], scale=1.0))
            A("scalar", lambda E: E.drain())
            A("scalar", lambda E: E.activation(
                fst[4][:, :], fst[4][:, :], AF.Identity,
                bias=0.0, scale=s_["s4b"][:, :]))
            sgd.append(s_done())
            W("vector", lm, lv)
            if it >= 2:
                A("vector", lambda E, h=so.h, v=oo_done[it - 2]: E.wait_ge(h, v))
            v_op(lambda E, uin=uin: E.tensor_scalar(fst[7][:, :], uin[:, :], s_["inv2b"][:, :],
                                                    MAGIC, ALU.mult, ALU.add))
            in_done.append(v_done())
            v_op(lambda E: E.tensor_scalar(fst[7][:, :], fst[7][:, :], 1.0, -MAGIC, ALU.mult, ALU.add))
            vdrain()
            v_op(lambda E: E.tensor_scalar(fst[7][:, :], fst[7][:, :], s_["s2b"][:, :], None, ALU.mult))
            vdrain()
            W("vector", sem["sd"], sgd[it])
            v_op(lambda E, oo=oo: E.tensor_tensor(oo[:, :], fst[4][:, :], fst[7][:, :], ALU.mult))
            vdrain()
            v_op(lambda E, oo=oo: E.tensor_reduce(s_["t1"][:, :], oo[:, :], AX.X, ALU.max,
                                                  apply_absolute_value=True))
            vdrain()
            v_op(lambda E: E.tensor_tensor(s_["run_o"][:, :], s_["run_o"][:, :], s_["t1"][:, :], ALU.max))
            v_o = v_done_f()
            v_prev.append(v_o)
            W("gpsimd", sem["vd"], v_o)
            dma_spill(oraw_d.ap()[:, it * TOK:(it + 1) * TOK], oo[:, :], sm=so)
            oo_done.append(so.n)

        v_op(lambda E: E.memset(cc_sb[0:1, :], 0.0))
        vdrain()
        xpart_max(s_["run_o"], 0)
        v_cc = v_done()
        W("gpsimd", sem["vd"], v_cc)
        g_ar = allreduce(3)
        W("vector", sem["gio"], g_ar)
        v_op(lambda E: E.tensor_scalar(s_["t1"][0:1, 0:1], cc_sb2[0:1, 0:1], 1.0 / QMAX, 1e-12,
                                       ALU.mult, ALU.max))
        vdrain()
        v_op(lambda E: E.reciprocal(s_["t3"][0:1, 0:1], s_["t1"][0:1, 0:1]))
        bcast128([s_["t1"], s_["t3"]], [s_["s5b"], s_["inv5b"]])
        barrier()

        if phase < 9:
            return _finish(nc, steps, sem, A, W, e)
        W("sync", sem["so0"], sem["so0"].n)
        W("sync", sem["so1"], sem["so1"].n)
        for c in range(24):
            dma_in(wdownT_d.ap()[:, c * D:(c + 1) * D], wdT_sb[:, c * D:(c + 1) * D])
        for c in range(8):
            dma_in(woT_d.ap()[:, c * D:(c + 1) * D], woT_sb[:, c * D:(c + 1) * D])
        l_wd = sem["lds"].n

        n_out = 0
        so_hist = []
        for half in range(2):
            h0 = half * 1024
            oh_done = []
            oh2_done = []
            W("sync", sem["vd"], sem["vd"].n)
            W("sync", sem["sd"], sem["sd"].n)
            for it in range(24):
                oin = ghalf[:, (it % 2) * 1024:(it % 2 + 1) * 1024]
                lo = sem["lo0"] if it % 2 == 0 else sem["lo1"]
                if it >= 2:
                    A("sync", lambda E, h=sem["sd"].h, v=oh_done[it - 2]: E.wait_ge(h, v))
                lv = dma_in(oraw_d.ap()[:, it * TOK + h0: it * TOK + h0 + 1024], oin,
                            sm=lo)
                W("scalar", lo, lv)
                if it >= 1:
                    W("scalar", sem["vd"], oh2_done[it - 1])
                A("scalar", lambda E, oin=oin: E.activation(
                    sig_scr[:, 0:1024], oin, AF.Identity,
                    bias=s_["magb"][:, :], scale=s_["inv5b"][:, :]))
                s_oh = s_done()
                oh_done.append(s_oh)
                W("vector", sem["sd"], s_oh)
                v_op(lambda E, it=it: E.tensor_scalar(ohat_sb[:, it * 1024:(it + 1) * 1024],
                                                      sig_scr[:, 0:1024], 1.0, -MAGIC,
                                                      ALU.mult, ALU.add))
                oh2_done.append(v_done_f())
            v_oh = v_done_f()
            W("tensor", sem["vd"], v_oh)
            W("tensor", sem["lds"], l_wd)
            for o in range(8):
                for j in range(2):
                    ps = psA if j == 0 else psB
                    for c in range(24):
                        A("tensor", lambda E, o=o, c=c, j=j, ps=ps: E.matmul(
                            ps[:, 0:512], wdT_sb[:, c * D + o * 128: c * D + (o + 1) * 128],
                            ohat_sb[:, c * 1024 + j * 512: c * 1024 + (j + 1) * 512],
                            start=(c == 0), stop=(c == 23)))
                t_dn = t_done()
                W("scalar", sem["td"], t_dn)
                for j in range(2):
                    ps = psA if j == 0 else psB
                    A("scalar", lambda E, j=j, ps=ps: E.activation(
                        sig_scr[:, j * 512:(j + 1) * 512], ps[:, 0:512], AF.Sigmoid,
                        bias=0.0, scale=s_["s5b"][:, :]))
                s_sg = s_done()
                W("vector", sem["sd"], s_sg)
                for j in range(2):
                    ps = psA if j == 0 else psB
                    v_op(lambda E, o=o, j=j, ps=ps: E.scalar_tensor_tensor(
                        ghalf[:, o * 1024 + j * 512: o * 1024 + (j + 1) * 512],
                        ps[:, 0:512], s_["s5b"][:, :], sig_scr[:, j * 512:(j + 1) * 512],
                        ALU.mult, ALU.mult))
                v_gh = v_done()
                W("tensor", sem["vd"], v_gh)
            for qtr in range(2):
                t0 = h0 + qtr * 512
                if half == 0 and qtr == 0:
                    W("sync", sem["ac0"], sem["ac0"].n)
                    W("sync", sem["ac1"], sem["ac1"].n)
                for c in range(8):
                    dma_in(acat_d.ap()[:, c * TOK + t0: c * TOK + t0 + 512],
                           acat_q[:, c * 512:(c + 1) * 512], sm=sem["lq"])
                for f in range(8):
                    dma_in(xn32_d.ap()[:, f * TOK + t0: f * TOK + t0 + 512],
                           xnq_q[:, f * 512:(f + 1) * 512], sm=sem["lq"])
                l_q = sem["lq"].n
                W("tensor", sem["lq"], l_q)
                vo_prev = None
                for o in range(8):
                    oc = (o % 2) * 512
                    if len(so_hist) >= 2:
                        W("tensor", sem["sd"], so_hist[-2])
                    for c in range(8):
                        A("tensor", lambda E, o=o, c=c, oc=oc: E.matmul(
                            psC[:, oc:oc + 512], woT_sb[:, c * D + o * 128: c * D + (o + 1) * 128],
                            acat_q[:, c * 512:(c + 1) * 512], start=(c == 0), stop=(c == 7)))
                    t_op = t_done()
                    W("scalar", sem["td"], t_op)
                    if vo_prev is not None:
                        W("scalar", sem["vd"], vo_prev)
                    A("scalar", lambda E, o=o, oc=oc: E.activation(
                        ot_scr[:, 0:512], psC[:, oc:oc + 512], AF.Identity,
                        bias=bcol["o"][:, o:o + 1], scale=1.0))
                    s_ot = s_done()
                    so_hist.append(s_ot)
                    W("vector", sem["sd"], s_ot)
                    v_op(lambda E, o=o: E.scalar_tensor_tensor(
                        ot_scr[:, 512:1024], ot_scr[:, 0:512], s_["rzb"][:, :],
                        xnq_q[:, o * 512:(o + 1) * 512], ALU.mult, ALU.add))
                    vdrain()
                    v_op(lambda E, o=o, qtr=qtr: E.tensor_tensor(
                        wgq[:, o * 512:(o + 1) * 512], ot_scr[:, 512:1024],
                        ghalf[:, o * 1024 + qtr * 512: o * 1024 + qtr * 512 + 512], ALU.mult))
                    vo_prev = v_done_f()
                v_wgq = v_done_f()
                W("tensor", sem["vd"], v_wgq)
                for tb in range(4):
                    lx = dma_in(xq_d.ap()[t0 + tb * 128: t0 + (tb + 1) * 128, :], xq_tok[:, :],
                                sm=sem["lx"])
                    for f in range(8):
                        A("tensor", lambda E, tb=tb, f=f: E.transpose(
                            psD[:, f * 128:(f + 1) * 128],
                            wgq[:, f * 512 + tb * 128: f * 512 + (tb + 1) * 128], ident[:, :]))
                    t_tr = t_done()
                    W("vector", sem["td"], t_tr)
                    W("vector", sem["lx"], lx)
                    if n_out >= 1:
                        W("vector", sem["st"], sem["st"].n)
                    v_op(lambda E: E.tensor_tensor(out_sb[:, :], psD[:, :], xq_tok[:, :], ALU.add))
                    v_out = v_done_f()
                    W("tensor", sem["vd"], v_out)
                    W("sync", sem["vd"], v_out)
                    A("sync", lambda E, t0=t0, tb=tb, h=sem["st"].h: E.dma_start(
                        out=y_d.ap()[t0 + tb * 128: t0 + (tb + 1) * 128, :],
                        in_=out_sb[:, :]).then_inc(h, 16))
                    sem["st"].inc(16)
                    n_out += 1
                    A("sync", lambda E, h=sem["st"].h, v=sem["st"].n: E.wait_ge(h, v))

        W("sync", sem["st"], sem["st"].n)
        return _finish(nc, steps, sem, A, W, e)


def _finish(nc, steps, sem, A, W, e):
    with nc.Block() as block:
        def mk_replay(name):
            def f(E):
                for eng2, fn in steps:
                    if eng2 == name:
                        fn(E)
            return f
        for name in ENGINES:
            getattr(block, name)(mk_replay(name))

    return nc


def kernel(**inputs):
    x = np.ascontiguousarray(inputs["L0_attn_x"], dtype=np.float32)
    in_maps = []
    for c in range(8):
        b, half = c // 2, c % 2
        m = {
            "xq": np.ascontiguousarray(x[b, half * TOK:(half + 1) * TOK]),
            "xk0": np.ascontiguousarray(x[b, 0:TOK]),
            "xk1": np.ascontiguousarray(x[b, TOK:SEQ]),
            "rez": np.ascontiguousarray(inputs["L0_attn_rezero"], dtype=np.float32),
            "w_gate": np.ascontiguousarray(inputs["L0_ffn_w_gate"], dtype=np.float32),
            "w_up": np.ascontiguousarray(inputs["L0_ffn_w_up"], dtype=np.float32),
            "w_down": np.ascontiguousarray(inputs["L0_ffn_w_down"], dtype=np.float32),
        }
        for p in "qkvo":
            m["W" + p] = np.ascontiguousarray(inputs[f"L0_attn_W{p}"], dtype=np.float32)
            m["b" + p] = np.ascontiguousarray(inputs[f"L0_attn_b{p}"], dtype=np.float32)
            m["A" + p] = np.ascontiguousarray(inputs[f"L0_attn_A{p}"], dtype=np.float32)
            m["B" + p] = np.ascontiguousarray(inputs[f"L0_attn_B{p}"], dtype=np.float32)
        in_maps.append(m)
    nc = build(8)
    res = run_bass_kernel_spmd(nc, in_maps, core_ids=list(range(8)))
    out = np.empty((4, SEQ, D), dtype=np.float32)
    for c in range(8):
        b, half = c // 2, c % 2
        out[b, half * TOK:(half + 1) * TOK] = res.results[c]["y"]
    return out


# version 21 (final)
# speedup vs baseline: 1.0315x; 1.0315x over previous
import numpy as np
from contextlib import ExitStack

import concourse.bass as bass
import concourse.mybir as mybir
from concourse.bass_utils import run_bass_kernel_spmd

F32 = mybir.dt.float32
BF16 = mybir.dt.bfloat16
AF = mybir.ActivationFunctionType
ALU = mybir.AluOpType
AX = mybir.AxisListType

D = 1024
DI = 3072
SEQ = 4096
TOK = 2048
H = 16
HD = 64
NHP = 8
MAGIC = float(1.5 * 2 ** 23)
CEXP = 12.0
EPS = 1e-5
QMAX = 32767.0
KB = 1024

ENGINES = ("sync", "vector", "scalar", "tensor", "gpsimd")


class Sem:
    __slots__ = ("h", "n")

    def __init__(self, h):
        self.h = h
        self.n = 0

    def inc(self, amt=1):
        self.n += amt
        return self.n


def build(n_cores=8, phase=99):
    import os
    phase = float(os.environ.get("KPHASE", phase))
    nc = bass.Bass("TRN2", target_bir_lowering=False)

    xq_d = nc.dram_tensor("xq", [TOK, D], F32, kind="ExternalInput")
    xk0_d = nc.dram_tensor("xk0", [TOK, D], F32, kind="ExternalInput")
    xk1_d = nc.dram_tensor("xk1", [TOK, D], F32, kind="ExternalInput")
    w_d = {}
    for p in "qkvo":
        w_d["W" + p] = nc.dram_tensor("W" + p, [D, D], F32, kind="ExternalInput")
        w_d["b" + p] = nc.dram_tensor("b" + p, [D], F32, kind="ExternalInput")
        w_d["A" + p] = nc.dram_tensor("A" + p, [8, D], F32, kind="ExternalInput")
        w_d["B" + p] = nc.dram_tensor("B" + p, [D, 8], F32, kind="ExternalInput")
    rez_d = nc.dram_tensor("rez", [1], F32, kind="ExternalInput")
    wg_d = nc.dram_tensor("w_gate", [DI, D], F32, kind="ExternalInput")
    wu_d = nc.dram_tensor("w_up", [DI, D], F32, kind="ExternalInput")
    wd_d = nc.dram_tensor("w_down", [D, DI], F32, kind="ExternalInput")
    y_d = nc.dram_tensor("y", [TOK, D], F32, kind="ExternalOutput")

    xn32_d = nc.dram_tensor("xn32_s", [128, 8 * TOK], F32)
    xhat_d = nc.dram_tensor("xhat_s", [128, 8 * TOK], BF16)
    wupT_d = nc.dram_tensor("wupT_s", [128, 8 * DI], BF16)
    wgateT_d = nc.dram_tensor("wgateT_s", [128, 8 * DI], BF16)
    wdownT_d = nc.dram_tensor("wdownT_s", [128, 24 * D], BF16)
    woT_d = nc.dram_tensor("woT_s", [128, 8 * D], BF16)
    wqT_d = nc.dram_tensor("wqT_s", [128, 8 * D], BF16)
    wkT_d = nc.dram_tensor("wkT_s", [128, 8 * D], BF16)
    wvT_d = nc.dram_tensor("wvT_s", [128, 8 * D], BF16)
    acat_d = nc.dram_tensor("acat_s", [128, NHP * TOK], BF16)
    upraw_d = nc.dram_tensor("upraw_s", [128, 24 * TOK], F32)
    gateraw_d = nc.dram_tensor("gateraw_s", [128, 24 * TOK], F32)
    m_d = nc.dram_tensor("m_s", [128, 24 * TOK], F32)
    oraw_d = nc.dram_tensor("oraw_s", [128, 24 * TOK], F32)
    cc_in = [nc.dram_tensor(f"ccin{i}", [1, 8], F32) for i in range(4)]
    cc_out = [nc.dram_tensor(f"ccout{i}", [1, 8], F32) for i in range(4)]

    SB_SZ = nc.SBUF_PARTITION_SIZE_BYTES
    base = (int(nc.sbuf_base) + 2048 + 31) & ~31
    TOPB = SB_SZ - 256
    placed = []

    def at(name, shape, dtype, off_kb, off_b=0):
        off = base + int(off_kb * KB) + off_b
        assert off % 32 == 0, name
        nbytes = int(np.prod(shape[1:])) * mybir.dt.size(dtype)
        assert off + nbytes <= TOPB, f"{name} overflow: {off}+{nbytes} vs {TOPB}"
        placed.append((name, off, off + nbytes))
        return nc.alloc_sbuf_tensor_at(name, list(shape), dtype, offset=off)

    ident = at("ident", [128, 128], F32, 0.0)
    ones_row = at("ones_row", [1, 128], F32, 0.5)
    ktab = at("ktab", [128, 32], F32, 1.0)
    biastab = at("biastab", [128, 512], F32, 1.25)
    bcol = {p: at("bcol_" + p, [128, 8], F32, 3.25, i * 32)
            for i, p in enumerate(["q8", "k", "v", "o"])}
    _sn = [0]

    def scal(name):
        t = at("sc_" + name, [128, 1], F32, 3.5, _sn[0] * 32)
        _sn[0] += 1
        assert _sn[0] <= 48
        return t

    s_ = {n: scal(n) for n in [
        "run1", "sum", "sumsq", "mean", "var", "t1", "t2", "t3", "rstd", "negmr",
        "s1", "inv1", "s1b", "inv1b",
        "run_up", "run_gt", "run_m", "run_o",
        "s2b", "inv2b", "s3b", "inv3b", "s4b", "inv4b", "s5b", "inv5b", "magb", "nmagb",
        "rz", "rzb"]}
    cc_sb = at("cc_sb", [1, 8], F32, 5.0)
    cc_sb2 = at("cc_sb2", [1, 8], F32, 5.0, 32)
    bv_row = at("bv_row", [1, D], F32, 5.5)

    ksb = at("ksb", [128, 8 * SEQ], BF16, 10, -128)
    vhat = at("vhat", [128, 32 * 16 * HD], BF16, 74)
    vones = at("vones", [128, HD], BF16, 9.5)
    qsb = at("qsb", [128, 8 * TOK], BF16, 139)
    pbuf = [at(f"pbuf{i}", [128, 1024], BF16, 171, i * 2048) for i in range(4)]
    acat = [at(f"acat{i}", [128, 512], BF16, 179, i * 1024) for i in range(2)]
    recipb = [at("recipb0", [128, 512], F32, 181), at("recipb1", [128, 512], F32, 183)]

    wstage = at("wstage", [128, 8192], F32, 10)
    wstage2 = at("wstage2", [128, 8192], F32, 102)
    wout = at("wout", [128, 24576], BF16, 42)
    bt32 = at("bt32", [8, 1024], F32, 90)
    ab16 = at("ab16", [8, 1024], BF16, 94)
    btb16 = at("btb16", [8, 1024], BF16, 96)
    wscr = at("wscr", [128, 1024], F32, 98)
    wq2 = at("wq2", [128, 8192], BF16, 10)
    wk2 = at("wk2", [128, 8192], BF16, 188)
    wv2 = at("wv2", [128, 8192], BF16, 188)
    xnq32 = at("xnq32", [128, 8 * TOK], F32, 74)
    xhat_sb = at("xhat_sb", [128, 8 * TOK], BF16, 10)
    uscr = at("uscr", [128, TOK], F32, 42)
    xnbq = at("xnbq", [128, 8 * TOK], BF16, 42)
    xnbk = at("xnbk", [128, 32 * 1024], BF16, 74)
    xtok = [at(f"xtok{i}", [128, D], F32, 170, i * 4096) for i in range(2)]
    sqscr = at("sqscr", [128, D], F32, 178)
    xnt = [at(f"xnt{i}", [128, D], F32, 182, i * 4096) for i in range(2)]

    fst = [at(f"fst{i}", [128, TOK], F32, 10, i * 8192) for i in range(8)]
    f1st = [at(f"f1st{i}", [128, TOK], F32, 42, i * 8192) for i in range(2)]
    wupT_sb = at("wupT_sb", [128, 8 * DI], BF16, 139)
    f2g = [at(f"f2g{i}", [128, 512], F32, 185, i * 2048) for i in range(4)]
    f2a = [at(f"f2a{i}", [128, 512], F32, 193, i * 2048) for i in range(4)]
    f2b = [at(f"f2b{i}", [128, 512], F32, 201, i * 2048) for i in range(2)]
    bvb = at("bvb", [128, 1024], F32, 181)
    ohat_sb = at("ohat_sb", [128, 24 * 1024], BF16, 10)
    wdT_sb = at("wdT_sb", [128, 24 * 1024], BF16, 58)
    ghalf = at("ghalf", [128, 8 * 1024], F32, 106)
    woT_sb = at("woT_sb", [128, 8 * D], BF16, 138)
    acat_q = at("acat_q", [128, 8 * 512], BF16, 154)
    xnq_q = at("xnq_q", [128, 8 * 512], F32, 162)
    ot_scr = at("ot_scr", [128, 1024], F32, 178)
    sig_scr = at("sig_scr", [128, 1024], F32, 182)
    wgq = at("wgq", [128, 4096], F32, 185.5)
    xq_tok = at("xq_tok", [128, D], F32, 201.5)
    out_sb = at("out_sb", [128, D], F32, 178)

    hi = max(e_ for _, _, e_ in placed)
    assert hi <= TOPB, f"layout overflow {hi} > {TOPB}"

    steps = []

    with ExitStack() as es:
        e = es.enter_context
        psA = e(nc.psum_tensor("psA", [128, 1024], F32))
        psB = e(nc.psum_tensor("psB", [128, 1024], F32))
        psC = e(nc.psum_tensor("psC", [128, 1024], F32))
        psD = e(nc.psum_tensor("psD", [128, 1024], F32))

        sem = {n: Sem(e(nc.semaphore("sm_" + n))) for n in [
            "lds", "spl", "gio", "gd", "vd", "sd", "td", "cc", "sc", "ex", "av", "st", "bar",
            "bcx", "ac0", "ac1", "fg0", "fg1", "fg2", "fg3",
            "lsm", "lw", "lw2", "ldx0", "ldx1", "lg0", "lg1", "lm0", "lm1", "lo0", "lo1",
            "lq", "lx", "sp0", "sp1", "so0", "so1"]}

        last_kind = {}
        last_op = {}
        tagged = set()

        def A(eng, fn):
            last_kind[eng] = "op"
            last_op[eng] = len(steps)
            steps.append((eng, fn))

        def _done(eng, s):
            i = last_op.get(eng)
            if i is None or i in tagged:
                A(eng, lambda E, h=s.h: E.drain().then_inc(h, 1))
                return s.inc()
            old = steps[i][1]
            steps[i] = (eng, lambda E, old=old, h=s.h: old(E).then_inc(h, 1))
            tagged.add(i)
            return s.inc()

        def W(eng, s, val=None):
            if last_kind.get(eng) == "wait":
                steps.append((eng, lambda E: E.drain()))
            last_kind[eng] = "wait"
            v = s.n if val is None else val
            steps.append((eng, lambda E, h=s.h, v=v: E.wait_ge(h, v)))

        def dma_in(src_ap, dst_ap, eng="sync", sm=None):
            s = sm if sm is not None else (sem["lds"] if eng == "sync" else sem["gio"])
            A(eng, lambda E, o=dst_ap, i=src_ap, h=s.h: E.dma_start(out=o, in_=i).then_inc(h, 16))
            return s.inc(16)

        def dma_spill(dst_ap, src_ap, sm=None):
            s = sm if sm is not None else sem["spl"]
            A("gpsimd", lambda E, o=dst_ap, i=src_ap, h=s.h: E.dma_start(out=o, in_=i).then_inc(h, 16))
            return s.inc(16)

        def v_op(fn):
            A("vector", fn)

        def v_done():
            A("vector", lambda E, h=sem["vd"].h: E.drain().then_inc(h, 1))
            return sem["vd"].inc()

        def v_done_f():
            return _done("vector", sem["vd"])

        def s_done():
            return _done("scalar", sem["sd"])

        def t_done():
            return _done("tensor", sem["td"])

        def vdrain():
            A("vector", lambda E: E.drain())

        _barn = [0]

        def barrier():
            _barn[0] += 1
            for eng in ("vector", "scalar", "tensor", "gpsimd"):
                A(eng, lambda E, h=sem["bar"].h: E.drain().then_inc(h, 1))
            A("sync", lambda E, h=sem["bar"].h: E.sem_inc(h, 1))
            tgt = sem["bar"].inc(5)
            for eng in ENGINES:
                W(eng, sem["bar"], tgt)

        A("gpsimd", lambda E: E.iota(ident[:, :], [[1, 128]], channel_multiplier=0,
                                     allow_small_or_imprecise_dtypes=True))
        A("gpsimd", lambda E: E.iota(sqscr[:, 0:128], [[0, 128]], channel_multiplier=1,
                                     allow_small_or_imprecise_dtypes=True))
        A("gpsimd", lambda E, h=sem["gd"].h: E.iota(ktab[:, :], [[128, 32]], channel_multiplier=1,
                                                    allow_small_or_imprecise_dtypes=True).then_inc(h, 1))
        g_iota = sem["gd"].inc()
        dma_in(rez_d.ap().rearrange("(a b) -> a b", a=1), s_["rz"][0:1, 0:1], sm=sem["lsm"])
        for p, dst in [("q", "q8"), ("k", "k"), ("v", "v"), ("o", "o")]:
            for i in range(8):
                src = w_d["b" + p].ap().rearrange("(a b) -> b a", a=8)[:, i:i + 1]
                dma_in(src, bcol[dst][:, i:i + 1], sm=sem["lsm"])
        dma_in(w_d["bv"].ap().rearrange("(a b) -> a b", a=1), bv_row[0:1, :], sm=sem["lsm"])
        l_small = sem["lsm"].n

        W("vector", sem["gd"], g_iota)
        v_op(lambda E: E.tensor_tensor(ident[:, :], ident[:, :], sqscr[:, 0:128], ALU.is_equal))
        vdrain()
        for h in range(H):
            sl = float(2.0 ** (-h))
            v_op(lambda E, h=h, sl=sl: E.tensor_scalar(
                biastab[:, h * 32:(h + 1) * 32], ktab[:, :], sl, -(sl * 4095.0 + CEXP),
                ALU.mult, ALU.add))
        v_op(lambda E: E.memset(ones_row[0:1, :], 1.0))
        v_op(lambda E: E.memset(vones[:, :], 1.0))
        v_op(lambda E: E.memset(s_["run1"][:, :], 0.0))
        v_op(lambda E: E.memset(s_["run_up"][:, :], 0.0))
        v_op(lambda E: E.memset(s_["run_gt"][:, :], 0.0))
        v_op(lambda E: E.memset(s_["run_m"][:, :], 0.0))
        v_op(lambda E: E.memset(s_["run_o"][:, :], 0.0))
        v_op(lambda E: E.memset(s_["magb"][:, :], MAGIC))
        v_op(lambda E: E.memset(s_["nmagb"][:, :], -MAGIC))
        W("vector", sem["lsm"], l_small)
        v_op(lambda E: E.tensor_scalar(bcol["q8"][:, :], bcol["q8"][:, :], 0.125, None, ALU.mult))
        v_setup = v_done()
        W("tensor", sem["vd"], v_setup)

        def bcast128(srcs, dsts):
            vw = v_done()
            W("tensor", sem["vd"], vw)
            for j, src in enumerate(srcs):
                A("tensor", lambda E, j=j, src=src: E.matmul(
                    psC[:, 512 + j:513 + j], ones_row[0:1, :], src[0:1, 0:1],
                    start=True, stop=True))
            tb = t_done()
            W("vector", sem["td"], tb)
            for j, dst in enumerate(dsts):
                v_op(lambda E, j=j, dst=dst: E.tensor_copy(dst[:, :], psC[:, 512 + j:513 + j]))
            vv = v_done()
            W("tensor", sem["vd"], vv)

        def allreduce(idx):
            A("gpsimd", lambda E, i=idx, h=sem["gio"].h: E.dma_start(
                out=cc_in[i].ap(), in_=cc_sb[0:1, :]).then_inc(h, 16))
            g1 = sem["gio"].inc(16)
            W("gpsimd", sem["gio"], g1)
            if n_cores > 1:
                A("gpsimd", lambda E, i=idx, h=sem["cc"].h: E.collective_compute(
                    "AllReduce", ALU.max, replica_groups=[list(range(n_cores))],
                    ins=[cc_in[i].ap().opt()], outs=[cc_out[i].ap().opt()]).then_inc(h, 1))
                c1 = sem["cc"].inc()
                W("gpsimd", sem["cc"], c1)
            else:
                A("gpsimd", lambda E, i=idx, h=sem["gio"].h: E.dma_start(
                    out=cc_out[i].ap(), in_=cc_in[i].ap()).then_inc(h, 16))
                W("gpsimd", sem["gio"], sem["gio"].inc(16))
            A("gpsimd", lambda E, i=idx, h=sem["gio"].h: E.dma_start(
                out=cc_sb2[0:1, :], in_=cc_out[i].ap()).then_inc(h, 16))
            return sem["gio"].inc(16)

        def xpart_max(src_scal, dst_col):
            vw = v_done()
            W("tensor", sem["vd"], vw)
            A("tensor", lambda E, s=src_scal: E.transpose(psD[0:1, 512:640], s[:, :], ident[:, :]))
            tt = t_done()
            W("vector", sem["td"], tt)
            v_op(lambda E, c=dst_col: E.tensor_reduce(cc_sb[0:1, c:c + 1], psD[0:1, 512:640],
                                                      AX.X, ALU.max))
            vv = v_done()
            W("tensor", sem["vd"], vv)

        wbufs = (wstage, wstage2)
        wbuf_tc = [None, None]
        wrnd = [0]

        def prep_weight(src_d, O, I, out_sbuf, out_dram, lora=None, scale=1.0, lpbq=False):
            oT, cT = O // 128, I // 128
            R = max(1, min(oT, 8192 // I))
            if lora is not None:
                A_d, B_d, lsc = lora
                W("vector", sem["td"], sem["td"].n)
                if wbuf_tc[0] is not None:
                    W("sync", sem["td"], wbuf_tc[0])
                for i in range(8):
                    dma_in(B_d.ap()[i * 128:(i + 1) * 128, :], wstage[:, i * 8:(i + 1) * 8],
                           sm=sem["lw"])
                lb = sem["lw"].n
                W("tensor", sem["lw"], lb)
                for i in range(8):
                    A("tensor", lambda E, i=i: E.transpose(psB[0:8, i * 128:(i + 1) * 128],
                                                           wstage[:, i * 8:(i + 1) * 8], ident[:, :]))
                tb = t_done()
                wbuf_tc[0] = tb
                W("vector", sem["td"], tb)
                v_op(lambda E, lsc=lsc: E.tensor_scalar(btb16[0:8, :], psB[0:8, :], lsc, None, ALU.mult))
                W("gpsimd", sem["td"], sem["td"].n)
                ga = dma_in(A_d.ap(), ab16[0:8, :], eng="gpsimd")
                vb = v_done()
                W("tensor", sem["vd"], vb)
                W("tensor", sem["gio"], ga)
            n_rounds = (oT + R - 1) // R
            lwv = {}

            def issue_loads(i):
                o0_, o1_ = i * R, min(oT, i * R + R)
                b = (wrnd[0] + i) % 2
                buf = wbufs[b]
                lsem = sem["lw"] if b == 0 else sem["lw2"]
                if wbuf_tc[b] is not None:
                    W("sync", sem["td"], wbuf_tc[b])
                for o in range(o0_, o1_):
                    dma_in(src_d.ap()[o * 128:(o + 1) * 128, :],
                           buf[:, (o - o0_) * I:(o - o0_ + 1) * I], sm=lsem)
                lwv[i] = lsem.n

            issue_loads(0)
            for rnd in range(n_rounds):
                o0, o1 = rnd * R, min(oT, rnd * R + R)
                if rnd + 1 < n_rounds:
                    issue_loads(rnd + 1)
                b = (wrnd[0] + rnd) % 2
                wstg = wbufs[b]
                lsem = sem["lw"] if b == 0 else sem["lw2"]
                lw = lwv[rnd]
                W("tensor", lsem, lw)
                if lpbq:
                    span = (o1 - o0) * I
                    nb = span // 32
                    W("vector", lsem, lw)
                    v_op(lambda E, nb=nb, wstg=wstg: E.tensor_reduce(
                        wscr[:, 0:nb],
                        bass.AP(wstg, 0, [[wstg.shape[1], 128], [32, nb], [1, 32]]),
                        AX.X, ALU.max, apply_absolute_value=True))
                    vdrain()
                    v_op(lambda E, nb=nb: E.tensor_scalar(
                        wscr[:, nb:2 * nb], wscr[:, 0:nb], 1.0 / 7.0, 1e-12,
                        ALU.mult, ALU.max))
                    vdrain()
                    v_op(lambda E, nb=nb: E.reciprocal(wscr[:, 2 * nb:3 * nb],
                                                       wscr[:, nb:2 * nb]))
                    vdrain()
                    v_op(lambda E, nb=nb, span=span, wstg=wstg: E.tensor_tensor(
                        wstg[:, 0:span], wstg[:, 0:span],
                        bass.AP(wscr, 2 * nb, [[wscr.shape[1], 128], [1, nb], [0, 32]]),
                        ALU.mult))
                    v_mul = v_done_f()
                    W("scalar", sem["vd"], v_mul)
                    A("scalar", lambda E, span=span, wstg=wstg: E.activation(
                        wstg[:, 0:span], wstg[:, 0:span], AF.Identity,
                        bias=s_["magb"][:, :], scale=1.0))
                    s_mg = s_done()
                    W("vector", sem["sd"], s_mg)
                    v_op(lambda E, nb=nb, span=span, wstg=wstg: E.scalar_tensor_tensor(
                        wstg[:, 0:span], wstg[:, 0:span], MAGIC,
                        bass.AP(wscr, nb, [[wscr.shape[1], 128], [1, nb], [0, 32]]),
                        ALU.subtract, ALU.mult))
                    v_lp = v_done_f()
                    W("tensor", sem["vd"], v_lp)
                last_tc = None
                for c in range(cT):
                    ps = psA if (c % 2 == 0) else psB
                    if lora is not None:
                        for j in range(2):
                            A("tensor", lambda E, c=c, j=j: E.matmul(
                                psC[:, j * 512:(j + 1) * 512], ab16[0:8, c * 128:(c + 1) * 128],
                                btb16[0:8, j * 512:(j + 1) * 512], start=True, stop=True))
                    for o in range(o0, o1):
                        A("tensor", lambda E, c=c, o=o, o0=o0, ps=ps, I=I, wstg=wstg: E.transpose(
                            ps[:, (o - o0) * 128:(o - o0 + 1) * 128],
                            wstg[:, (o - o0) * I + c * 128:(o - o0) * I + (c + 1) * 128],
                            ident[:, :]))
                    tc = t_done()
                    last_tc = tc
                    ncols = (o1 - o0) * 128
                    if lora is not None:
                        W("scalar", sem["td"], tc)
                        A("scalar", lambda E, ps=ps, n=ncols: E.copy(wscr[:, 0:n], ps[:, 0:n]))
                        sc_t = s_done()
                        W("vector", sem["sd"], sc_t)
                        v_op(lambda E, c=c, n=ncols, sc=scale, O=O: E.scalar_tensor_tensor(
                            out_sbuf[:, c * O: c * O + n], wscr[:, 0:n], sc, psC[:, 0:n],
                            ALU.mult, ALU.add))
                    elif c % 2 == 0:
                        W("vector", sem["td"], tc)
                        v_op(lambda E, c=c, o0=o0, n=ncols, ps=ps, O=O: E.tensor_copy(
                            out_sbuf[:, c * O + o0 * 128: c * O + o0 * 128 + n], ps[:, 0:n]))
                    else:
                        W("scalar", sem["td"], tc)
                        A("scalar", lambda E, c=c, o0=o0, n=ncols, ps=ps, O=O: E.copy(
                            out_sbuf[:, c * O + o0 * 128: c * O + o0 * 128 + n], ps[:, 0:n]))
                    if lora is not None or c % 2 == 0:
                        vc = v_done_f()
                        W("tensor", sem["vd"], vc)
                    else:
                        sc2 = s_done()
                        W("tensor", sem["sd"], sc2)
                wbuf_tc[b] = last_tc
            wrnd[0] += n_rounds
            vs = v_done_f()
            if out_dram is not None:
                W("gpsimd", sem["vd"], vs)
                W("gpsimd", sem["sd"], sem["sd"].n)
                for c in range(cT):
                    dma_spill(out_dram.ap()[:, c * O:(c + 1) * O],
                              out_sbuf[:, c * O:(c + 1) * O])
                sp = sem["spl"].n
                W("vector", sem["spl"], sp)
                W("scalar", sem["spl"], sp)

        prep_weight(w_d["Wq"], D, D, wout, wqT_d, lora=(w_d["Aq"], w_d["Bq"], 1.0 / 64.0), scale=0.125)
        prep_weight(w_d["Wk"], D, D, wout, wkT_d, lora=(w_d["Ak"], w_d["Bk"], 0.125))
        prep_weight(w_d["Wv"], D, D, wout, wvT_d, lora=(w_d["Av"], w_d["Bv"], 0.125))
        prep_weight(w_d["Wo"], D, D, wout, woT_d, lora=(w_d["Ao"], w_d["Bo"], 0.125))
        prep_weight(wu_d, DI, D, wout, wupT_d, lpbq=True)
        prep_weight(wg_d, DI, D, wout, wgateT_d, lpbq=True)
        prep_weight(wd_d, D, DI, wout, wdownT_d, lpbq=True)
        barrier()

        if phase < 2:
            return _finish(nc, steps, sem, A, W, e)
        def ln_tiles(src_d, nt, dst32, dstbf, track_max):
            norm_done = []
            tr_done = []
            evac_done = []
            W("sync", sem["sd"], sem["sd"].n)
            for i in range(nt):
                xt = xtok[i % 2]
                ldx = sem["ldx0"] if i % 2 == 0 else sem["ldx1"]
                if i >= 2:
                    A("sync", lambda E, h=sem["sd"].h, v=norm_done[i - 2]: E.wait_ge(h, v))
                lv = dma_in(src_d.ap()[i * 128:(i + 1) * 128, :], xt[:, :], sm=ldx)
                W("scalar", ldx, lv)
                W("vector", ldx, lv)
                v_op(lambda E, xt=xt: E.bn_stats(sqscr[:, 0:6], xt[:, 0:512]))
                v_op(lambda E, xt=xt: E.bn_stats(sqscr[:, 6:12], xt[:, 512:1024]))
                vdrain()
                v_op(lambda E: E.bn_aggr(sqscr[:, 12:14], sqscr[:, 0:12]))
                vdrain()
                v_op(lambda E: E.tensor_scalar(s_["t2"][:, :], sqscr[:, 13:14], 1.0, EPS,
                                               ALU.mult, ALU.add))
                v_var = v_done()
                W("scalar", sem["vd"], v_var)
                A("scalar", lambda E: E.activation(s_["t2"][:, :], s_["t2"][:, :], AF.Sqrt,
                                                   bias=0.0, scale=1.0))
                s_rs = s_done()
                W("vector", sem["sd"], s_rs)
                v_op(lambda E: E.reciprocal(s_["rstd"][:, :], s_["t2"][:, :]))
                vdrain()
                v_op(lambda E: E.tensor_tensor(s_["negmr"][:, :], sqscr[:, 12:13], s_["rstd"][:, :], ALU.mult))
                vdrain()
                v_op(lambda E: E.tensor_scalar(s_["negmr"][:, :], s_["negmr"][:, :], -1.0, None, ALU.mult))
                v_stats = v_done()
                xo = xnt[i % 2]
                W("scalar", sem["vd"], v_stats)
                if i >= 2:
                    A("scalar", lambda E, h=sem["td"].h, v=tr_done[i - 2]: E.wait_ge(h, v))
                A("scalar", lambda E, xt=xt, xo=xo: E.activation(xo[:, :], xt[:, :], AF.Identity,
                                                                bias=s_["negmr"][:, :],
                                                                scale=s_["rstd"][:, :]))
                s_norm = s_done()
                norm_done.append(s_norm)
                if track_max is not None:
                    W("vector", sem["sd"], s_norm)
                    v_op(lambda E, xo=xo: E.tensor_reduce(s_["t1"][:, :], xo[:, :], AX.X, ALU.max,
                                                          apply_absolute_value=True))
                    vdrain()
                    v_op(lambda E, tm=track_max: E.tensor_tensor(tm[:, :], tm[:, :], s_["t1"][:, :], ALU.max))
                    vdrain()
                ps = psA if (i % 2 == 0) else psB
                W("tensor", sem["sd"], s_norm)
                if i >= 2:
                    A("tensor", lambda E, h=sem["vd"].h, v=evac_done[i - 2][0]: E.wait_ge(h, v))
                    A("tensor", lambda E, h=sem["sd"].h, v=evac_done[i - 2][1]: E.wait_ge(h, v))
                for f in range(8):
                    A("tensor", lambda E, f=f, ps=ps, xo=xo: E.transpose(
                        ps[:, f * 128:(f + 1) * 128], xo[:, f * 128:(f + 1) * 128], ident[:, :]))
                t_tr = t_done()
                tr_done.append(t_tr)
                ve, se2 = sem["vd"].n, sem["sd"].n
                if dst32 is not None:
                    dst, stride, toff = dst32
                    W("vector", sem["td"], t_tr)
                    v_op(lambda E, i=i, ps=ps, dst=dst, stride=stride, toff=toff: E.tensor_copy(
                        bass.AP(dst, toff + i * 128, [[dst.shape[1], 128], [stride, 8], [1, 128]]),
                        ps[:, :]))
                    ve = v_done()
                if dstbf is not None:
                    dst, offfn = dstbf
                    co, fs = offfn(i)
                    W("scalar", sem["td"], t_tr)
                    A("scalar", lambda E, ps=ps, dst=dst, co=co, fs=fs: E.activation(
                        bass.AP(dst, co, [[dst.shape[1], 128], [fs, 8], [1, 128]]),
                        ps[:, :], AF.Identity, bias=0.0, scale=1.0))
                    se2 = s_done()
                evac_done.append((ve, se2))
            W("tensor", sem["vd"], sem["vd"].n)
            W("tensor", sem["sd"], sem["sd"].n)

        ln_tiles(xq_d, 16, (xnq32, TOK, 0), None, s_["run1"])

        v_op(lambda E: E.memset(cc_sb[0:1, :], 0.0))
        vdrain()
        xpart_max(s_["run1"], 0)
        v_cc = v_done()
        W("gpsimd", sem["vd"], v_cc)
        g_ar = allreduce(0)
        W("vector", sem["gio"], g_ar)
        v_op(lambda E: E.tensor_scalar(s_["s1"][0:1, :], cc_sb2[0:1, 0:1], 1.0 / QMAX, 1e-12,
                                       ALU.mult, ALU.max))
        vdrain()
        v_op(lambda E: E.reciprocal(s_["inv1"][0:1, :], s_["s1"][0:1, :]))
        bcast128([s_["s1"], s_["inv1"], s_["rz"]], [s_["s1b"], s_["inv1b"], s_["rzb"]])

        for f in range(8):
            v_op(lambda E, f=f: E.tensor_scalar(uscr[:, :], xnq32[:, f * TOK:(f + 1) * TOK],
                                                s_["inv1b"][:, :], MAGIC, ALU.mult, ALU.add))
            vdrain()
            v_op(lambda E, f=f: E.tensor_scalar(xhat_sb[:, f * TOK:(f + 1) * TOK], uscr[:, :],
                                                1.0, -MAGIC, ALU.mult, ALU.add))
            vdrain()
        v_xh = v_done()
        W("gpsimd", sem["vd"], v_xh)
        dma_spill(xn32_d.ap(), xnq32[:, :])
        barrier()

        if phase < 3:
            return _finish(nc, steps, sem, A, W, e)
        W("scalar", sem["spl"], sem["spl"].n)
        ln_tiles(xk0_d, 16, None, (xnbk, lambda i: (i * 1024, 128)), None)
        ln_tiles(xk1_d, 16, None, (xnbk, lambda i: ((i + 16) * 1024, 128)), None)
        barrier()


        if phase < 3.5:
            return _finish(nc, steps, sem, A, W, e)
        W("gpsimd", sem["spl"], sem["spl"].n)
        W("sync", sem["spl"], sem["spl"].n)
        for c in range(8):
            dma_in(wupT_d.ap()[:, c * DI:(c + 1) * DI], wupT_sb[:, c * DI:(c + 1) * DI])
        l_w1 = sem["lds"].n
        W("tensor", sem["lds"], l_w1)

        def f1_weight(wT, out_dram, runmax, nit=24):
            A("scalar", lambda E, h=sem["sp0"].h, v=sem["sp0"].n: E.wait_ge(h, v))
            A("scalar", lambda E, h=sem["sp1"].h, v=sem["sp1"].n: E.wait_ge(h, v))
            stg_done = []
            grp_done = []
            for it in range(nit):
                stg = f1st[it % 2]
                sp = sem["sp0"] if it % 2 == 0 else sem["sp1"]
                if it >= 2:
                    A("scalar", lambda E, h=sp.h, v=stg_done[it - 2]: E.wait_ge(h, v))
                for t0 in range(0, TOK, 512):
                    ps = psA if ((t0 // 512) % 2 == 0) else psB
                    if len(grp_done) >= 2:
                        sfp, vvp = grp_done[-2]
                        W("tensor", sem["sd"], sfp)
                        W("tensor", sem["vd"], vvp)
                    for c in range(8):
                        A("tensor", lambda E, it=it, c=c, t0=t0, ps=ps, wT=wT: E.matmul(
                            ps[:, 0:512], wT[:, c * DI + it * 128: c * DI + (it + 1) * 128],
                            xhat_sb[:, c * TOK + t0: c * TOK + t0 + 512],
                            start=(c == 0), stop=(c == 7)))
                    tf = t_done()
                    W("scalar", sem["td"], tf)
                    A("scalar", lambda E, t0=t0, ps=ps, stg=stg: E.activation(
                        stg[:, t0:t0 + 512], ps[:, 0:512], AF.Identity,
                        bias=0.0, scale=s_["s1b"][:, :]))
                    sf = s_done()
                    W("vector", sem["sd"], sf)
                    tsl = s_["t1"] if (t0 // 512) % 2 == 0 else s_["t2"]
                    v_op(lambda E, ps=ps, tsl=tsl: E.tensor_reduce(tsl[:, :], ps[:, 0:512], AX.X,
                                                                   ALU.max, apply_absolute_value=True))
                    vdrain()
                    v_op(lambda E, rm=runmax, tsl=tsl: E.tensor_tensor(rm[:, :], rm[:, :], tsl[:, :], ALU.max))
                    vv = v_done_f()
                    grp_done.append((sf, vv))
                s_stg = sem["sd"].n
                W("gpsimd", sem["sd"], s_stg)
                dma_spill(out_dram.ap()[:, it * TOK:(it + 1) * TOK], stg[:, :], sm=sp)
                stg_done.append(sp.n)
            W("tensor", sem["sd"], sem["sd"].n)
            W("tensor", sem["vd"], sem["vd"].n)

        f1_weight(wupT_sb, upraw_d, s_["run_up"])
        W("sync", sem["td"], sem["td"].n)
        for c in range(8):
            dma_in(wgateT_d.ap()[:, c * DI:(c + 1) * DI], wupT_sb[:, c * DI:(c + 1) * DI])
        W("tensor", sem["lds"], sem["lds"].n)
        f1_weight(wupT_sb, gateraw_d, s_["run_gt"])

        v_op(lambda E: E.memset(cc_sb[0:1, :], 0.0))
        vdrain()
        xpart_max(s_["run_up"], 0)
        xpart_max(s_["run_gt"], 1)
        v_cc = v_done()
        W("gpsimd", sem["vd"], v_cc)
        g_ar = allreduce(1)
        W("vector", sem["gio"], g_ar)
        for col, sn, invn in ((0, "s2b", "inv2b"), (1, "s3b", "inv3b")):
            v_op(lambda E, col=col: E.tensor_scalar(s_["t1"][0:1, 0:1], cc_sb2[0:1, col:col + 1],
                                                    s_["s1"][0:1, 0:1], None, ALU.mult))
            vdrain()
            v_op(lambda E: E.tensor_scalar(s_["t1"][0:1, 0:1], s_["t1"][0:1, 0:1], 1.0 / QMAX,
                                           1e-12, ALU.mult, ALU.max))
            vdrain()
            v_op(lambda E: E.reciprocal(s_["t3"][0:1, 0:1], s_["t1"][0:1, 0:1]))
            bcast128([s_["t1"], s_["t3"]], [s_[sn], s_[invn]])
        barrier()

        if phase < 4:
            return _finish(nc, steps, sem, A, W, e)
        W("gpsimd", sem["spl"], sem["spl"].n)
        W("gpsimd", sem["sp0"], sem["sp0"].n)
        W("gpsimd", sem["sp1"], sem["sp1"].n)
        A("gpsimd", lambda E, h=sem["gio"].h: E.dma_start(out=xnbq[:, :], in_=xn32_d.ap()).then_inc(h, 16))
        g_xnbq = sem["gio"].inc(16)

        W("sync", sem["spl"], sem["spl"].n)
        W("sync", sem["sp0"], sem["sp0"].n)
        W("sync", sem["sp1"], sem["sp1"].n)
        l_wq = dma_in(wqT_d.ap(), wq2[:, :])
        W("tensor", sem["lds"], l_wq)
        W("tensor", sem["gio"], g_xnbq)
        pj_hist = []
        for o in range(8):
            for t0 in range(0, TOK, 512):
                ps = psA if (((o * 4) + t0 // 512) % 2 == 0) else psB
                if len(pj_hist) >= 2:
                    W("tensor", sem["sd"], pj_hist[-2])
                for c in range(8):
                    A("tensor", lambda E, o=o, c=c, t0=t0, ps=ps: E.matmul(
                        ps[:, 0:512], wq2[:, c * D + o * 128: c * D + (o + 1) * 128],
                        xnbq[:, c * TOK + t0: c * TOK + t0 + 512],
                        start=(c == 0), stop=(c == 7)))
                tp = t_done()
                W("scalar", sem["td"], tp)
                A("scalar", lambda E, o=o, t0=t0, ps=ps: E.activation(
                    qsb[:, o * TOK + t0: o * TOK + t0 + 512], ps[:, 0:512], AF.Identity,
                    bias=bcol["q8"][:, o:o + 1], scale=1.0))
                pj_hist.append(s_done())

        W("sync", sem["td"], sem["td"].n)
        l_wk = dma_in(wkT_d.ap(), wk2[:, :])
        W("tensor", sem["lds"], l_wk)
        for o in range(8):
            for t0 in range(0, SEQ, 512):
                ps = psA if (((o * 8) + t0 // 512) % 2 == 0) else psB
                if len(pj_hist) >= 2:
                    W("tensor", sem["sd"], pj_hist[-2])
                for c in range(8):
                    A("tensor", lambda E, o=o, c=c, t0=t0, ps=ps: E.matmul(
                        ps[:, 0:512],
                        wk2[:, c * D + o * 128: c * D + (o + 1) * 128],
                        bass.AP(xnbk, (t0 // 128) * 1024 + c * 128,
                                [[xnbk.shape[1], 128], [1024, 4], [1, 128]]),
                        start=(c == 0), stop=(c == 7)))
                tp = t_done()
                W("scalar", sem["td"], tp)
                A("scalar", lambda E, o=o, t0=t0, ps=ps: E.activation(
                    ksb[:, o * SEQ + t0: o * SEQ + t0 + 512], ps[:, 0:512], AF.Identity,
                    bias=bcol["k"][:, o:o + 1], scale=1.0))
                pj_hist.append(s_done())

        W("tensor", sem["sd"], sem["sd"].n)
        vw = v_done()
        W("tensor", sem["vd"], vw)
        for j in range(2):
            A("tensor", lambda E, j=j: E.matmul(psC[:, j * 512:(j + 1) * 512], ones_row[0:1, :],
                                                bv_row[0:1, j * 512:(j + 1) * 512],
                                                start=True, stop=True))
        t_bv = t_done()
        W("vector", sem["td"], t_bv)
        v_op(lambda E: E.tensor_copy(bvb[:, :], psC[:, :]))
        v_bv = v_done()
        W("tensor", sem["vd"], v_bv)

        W("sync", sem["td"], sem["td"].n)
        l_wv = dma_in(wvT_d.ap(), wv2[:, :])
        W("tensor", sem["lds"], l_wv)
        vv_hist = []
        for tt in range(32):
            ps = psA if (tt % 2 == 0) else psB
            if len(vv_hist) >= 2:
                W("tensor", sem["vd"], vv_hist[-2])
            for c in range(8):
                for j in range(2):
                    A("tensor", lambda E, tt=tt, c=c, j=j, ps=ps: E.matmul(
                        ps[:, j * 512:(j + 1) * 512],
                        xnbk[:, tt * 1024 + c * 128: tt * 1024 + (c + 1) * 128],
                        wv2[:, c * D + j * 512: c * D + (j + 1) * 512],
                        start=(c == 0), stop=(c == 7)))
            tv = t_done()
            W("vector", sem["td"], tv)
            v_op(lambda E, tt=tt, ps=ps: E.scalar_tensor_tensor(
                bass.AP(vhat, tt * 1024, [[vhat.shape[1], 128], [HD, 16], [1, HD]]),
                ps[:, :], 1.0, bvb[:, :], ALU.mult, ALU.add))
            vv_hist.append(v_done_f())
        W("tensor", sem["vd"], sem["vd"].n)
        barrier()

        if phase < 5:
            return _finish(nc, steps, sem, A, W, e)
        skipB = os.environ.get("KSKIPB", "") == "1"
        first = True
        tt_hist = {}
        bc_hist = {}
        spill_hist = {}
        rc_hist = {}

        NF2 = 96
        f2_loads = {}
        f2_vg = {}
        f2_sig = {}
        f2_vm = {}
        f2_vm2 = {}
        f2_spill = {}
        f2_p1 = [0]
        f2_p2 = [0]

        def f2_col(ch):
            it_, qt = divmod(ch, 4)
            return it_ * TOK + qt * 512

        def f2_lg(ch):
            return sem["fg%d" % (ch % 4)]

        def f2_so(ch):
            return sem[["so0", "so1", "sp0", "sp1"][ch % 4]]

        def f2_issue_load(ch):
            if ch >= NF2 or skipB or ch in f2_loads:
                return
            lg = f2_lg(ch)
            so = f2_so(ch)
            if ch >= 4:
                A("gpsimd", lambda E, h=so.h, v=f2_spill[ch - 4]: E.wait_ge(h, v))
            f2_loads[ch] = dma_in(
                gateraw_d.ap()[:, f2_col(ch): f2_col(ch) + 512],
                f2g[ch % 4][:, :], eng="gpsimd", sm=lg)

        def f2_phase1(ch):
            gin, scrA = f2g[ch % 4], f2a[ch % 4]
            W("vector", f2_lg(ch), f2_loads[ch])
            if ch >= 4:
                W("vector", sem["sd"], f2_sig[ch - 4])
            v_op(lambda E, gin=gin: E.tensor_scalar(gin[:, :], gin[:, :], s_["inv3b"][:, :],
                                                    MAGIC, ALU.mult, ALU.add))
            vdrain()
            v_op(lambda E, gin=gin: E.tensor_scalar(gin[:, :], gin[:, :], 1.0, -MAGIC,
                                                    ALU.mult, ALU.add))
            vdrain()
            v_op(lambda E, gin=gin, scrA=scrA: E.tensor_scalar(scrA[:, :], gin[:, :],
                                                               s_["s3b"][:, :], None, ALU.mult))
            f2_vg[ch] = v_done_f()

        def f2_phase2(ch):
            gin, scrA, scrB = f2g[ch % 4], f2a[ch % 4], f2b[ch % 2]
            so = f2_so(ch)
            W("scalar", sem["vd"], f2_vg[ch])
            if ch >= 2:
                W("scalar", sem["vd"], f2_vm2.get(ch - 2, 0))
            A("scalar", lambda E, scrA=scrA, scrB=scrB: E.activation(
                scrB[:, :], scrA[:, :], AF.Sigmoid, bias=0.0, scale=1.0))
            f2_sig[ch] = s_done()
            W("vector", sem["sd"], f2_sig[ch])
            v_op(lambda E, scrB=scrB: E.tensor_scalar(scrB[:, :], scrB[:, :], 65536.0, MAGIC,
                                                      ALU.mult, ALU.add))
            vdrain()
            v_op(lambda E, scrB=scrB: E.tensor_scalar(scrB[:, :], scrB[:, :], 1.0, -MAGIC,
                                                      ALU.mult, ALU.add))
            vdrain()
            v_op(lambda E, scrB=scrB: E.tensor_scalar(scrB[:, :], scrB[:, :], float(2.0 ** -16),
                                                      None, ALU.mult))
            vdrain()
            v_op(lambda E, gin=gin, scrA=scrA, scrB=scrB: E.tensor_tensor(
                gin[:, :], scrA[:, :], scrB[:, :], ALU.mult))
            f2_vm2[ch] = v_done_f()
            vdrain()
            v_op(lambda E, gin=gin: E.tensor_reduce(s_["t1"][:, :], gin[:, :], AX.X, ALU.max,
                                                    apply_absolute_value=True))
            vdrain()
            v_op(lambda E: E.tensor_tensor(s_["run_m"][:, :], s_["run_m"][:, :],
                                           s_["t1"][:, :], ALU.max))
            f2_vm[ch] = v_done_f()
            W("gpsimd", sem["vd"], f2_vm[ch])
            dma_spill(m_d.ap()[:, f2_col(ch): f2_col(ch) + 512], gin[:, :], sm=so)
            f2_spill[ch] = so.n
            f2_issue_load(ch + 4)

        if not skipB:
            W("gpsimd", sem["sp0"], sem["sp0"].n)
            W("gpsimd", sem["sp1"], sem["sp1"].n)
            for _c in range(4):
                f2_issue_load(_c)

        def norm_tt(idx):
            half = idx % 2
            ac = acat[idx % 2]
            sp = sem["ac0"] if idx % 2 == 0 else sem["ac1"]
            if idx >= 2:
                W("vector", sp, spill_hist[idx - 2])
            W("vector", sem["bcx"], bc_hist[idx])
            for j, psav in enumerate((psC, psD)):
                v_op(lambda E, j=j, psav=psav, ac=ac, half=half: E.tensor_tensor(
                    ac[j * 64:(j + 1) * 64, 0:512],
                    psav[0:64, half * 512: half * 512 + 512],
                    recipb[j][0:64, 0:512], ALU.mult))
            tt_hist[idx] = v_done_f()
            W("gpsimd", sem["vd"], tt_hist[idx])
            hp_, qb_ = divmod(idx, 4)
            dma_spill(acat_d.ap()[:, hp_ * TOK + qb_ * 512: hp_ * TOK + qb_ * 512 + 512],
                      ac[:, 0:512], sm=sp)
            spill_hist[idx] = sp.n

        KT0 = []
        for hp in range(NHP):
            smin = 2.0 ** (-(2 * hp + 1))
            k0 = 0
            while k0 < 32 and smin * (4095.0 - (k0 * 128 + 127)) > 64.0:
                k0 += 1
            KT0.append(k0)

        for hp in range(NHP if not skipB else 0):
            kt0 = KT0[hp]
            for qb in range(4):
                idx = hp * 4 + qb
                half = idx % 2
                q0 = qb * 512
                exp_hist = []
                for kt in range(kt0, 33):
                    nk = kt - kt0
                    if kt < 32:
                        par = kt % 2
                        if nk >= 2:
                            W("tensor", sem["ex"], exp_hist[nk - 2])
                        elif not first:
                            W("tensor", sem["ex"], sem["ex"].n - 1)
                        for h, ps in ((0, psA), (1, psB)):
                            A("tensor", lambda E, hp=hp, h=h, q0=q0, kt=kt, par=par, ps=ps: E.matmul(
                                ps[:, par * 512:(par + 1) * 512],
                                ksb[h * 64:(h + 1) * 64,
                                    hp * SEQ + kt * 128: hp * SEQ + (kt + 1) * 128],
                                qsb[h * 64:(h + 1) * 64,
                                    hp * TOK + q0: hp * TOK + q0 + 512],
                                start=True, stop=True, tile_position=(h * 64, 0)))
                        _done("tensor", sem["sc"])
                    if kt >= kt0 + 1:
                        ktm = kt - 1
                        parm = ktm % 2
                        pb1, pb2 = pbuf[parm * 2], pbuf[parm * 2 + 1]
                        W("tensor", sem["ex"], sem["ex"].n)
                        if ktm == kt0 and idx >= 2:
                            W("tensor", sem["vd"], tt_hist[idx - 2])
                        for pb, psav, h in ((pb1, psC, 2 * hp), (pb2, psD, 2 * hp + 1)):
                            oo = ktm * 1024 + h * 64
                            A("tensor", lambda E, oo=oo, pb=pb, psav=psav, ktm=ktm, half=half, kt0=kt0: E.matmul(
                                psav[0:64, half * 512: half * 512 + 512],
                                bass.AP(vhat, oo, [[vhat.shape[1], 128], [1, 64]]),
                                pb[:, 0:512],
                                start=(ktm == kt0), stop=(ktm == 31)))
                            A("tensor", lambda E, pb=pb, psav=psav, ktm=ktm, half=half, kt0=kt0: E.matmul(
                                psav[64:128, (1 - half) * 512: (1 - half) * 512 + 512],
                                vones[:, 0:HD], pb[:, 0:512],
                                start=(ktm == kt0), stop=(ktm == 31)))
                        _done("tensor", sem["av"])
                    if kt < 32:
                        first = False
                        W("scalar", sem["sc"], sem["sc"].n)
                        if nk >= 2:
                            W("scalar", sem["av"], sem["av"].n - 1)
                        par = kt % 2
                        pb1, pb2 = pbuf[par * 2], pbuf[par * 2 + 1]
                        A("scalar", lambda E, hp=hp, kt=kt, par=par, pb1=pb1: E.activation(
                            pb1[:, 0:512], psA[:, par * 512:(par + 1) * 512], AF.Exp,
                            bias=biastab[:, (2 * hp) * 32 + kt:(2 * hp) * 32 + kt + 1],
                            scale=1.0))
                        A("scalar", lambda E, hp=hp, kt=kt, par=par, pb2=pb2: E.activation(
                            pb2[:, 0:512], psB[:, par * 512:(par + 1) * 512], AF.Exp,
                            bias=biastab[:, (2 * hp + 1) * 32 + kt:(2 * hp + 1) * 32 + kt + 1],
                            scale=1.0))
                        exp_hist.append(_done("scalar", sem["ex"]))
                W("vector", sem["av"], sem["av"].n)
                if idx >= 1:
                    W("vector", sem["bcx"], bc_hist[idx - 1])
                for j, psav in enumerate((psC, psD)):
                    v_op(lambda E, j=j, psav=psav, half=half: E.reciprocal(
                        recipb[j][64:65, 0:512],
                        psav[64:65, (1 - half) * 512: (1 - half) * 512 + 512]))
                rc_hist[idx] = v_done_f()
                if idx >= 1:
                    norm_tt(idx - 1)
                W("gpsimd", sem["vd"], tt_hist[idx - 1] if idx >= 1 else rc_hist[idx])
                for j in range(2):
                    A("gpsimd", lambda E, j=j, h=sem["bcx"].h: E.dma_start(
                        out=recipb[j][0:64, 0:512],
                        in_=bass.AP(recipb[j], 64 * recipb[j].shape[1],
                                    [[recipb[j].shape[1], 1], [0, 64], [1, 512]])
                    ).then_inc(h, 16))
                    sem["bcx"].inc(16)
                bc_hist[idx] = sem["bcx"].n
                loaded = len(f2_loads)
                while f2_p2[0] < f2_p1[0]:
                    f2_phase2(f2_p2[0])
                    f2_p2[0] += 1
                while f2_p1[0] < min(NF2, loaded, (idx + 1) * 3):
                    f2_phase1(f2_p1[0])
                    f2_p1[0] += 1
        if not skipB:
            norm_tt(NHP * 4 - 1)
            while f2_p1[0] < NF2:
                f2_issue_load(f2_p1[0])
                f2_phase1(f2_p1[0])
                f2_p1[0] += 1
                while f2_p2[0] < max(0, f2_p1[0] - 2):
                    f2_phase2(f2_p2[0])
                    f2_p2[0] += 1
            while f2_p2[0] < NF2:
                f2_phase2(f2_p2[0])
                f2_p2[0] += 1
        W("tensor", sem["vd"], sem["vd"].n)
        for _ps in (psA, psB, psC, psD):
            A("tensor", lambda E, _ps=_ps: E.matmul(_ps[0:64, 0:512], vones[:, 0:64],
                                                    qsb[:, 0:512], start=True, stop=True))
        A("tensor", lambda E: E.drain())
        barrier()

        if phase < 6:
            return _finish(nc, steps, sem, A, W, e)
        v_op(lambda E: E.memset(cc_sb[0:1, :], 0.0))
        vdrain()
        xpart_max(s_["run_m"], 0)
        v_cc = v_done()
        W("gpsimd", sem["vd"], v_cc)
        g_ar = allreduce(2)
        W("vector", sem["gio"], g_ar)
        v_op(lambda E: E.tensor_scalar(s_["t1"][0:1, 0:1], cc_sb2[0:1, 0:1], 1.0 / QMAX, 1e-12,
                                       ALU.mult, ALU.max))
        vdrain()
        v_op(lambda E: E.reciprocal(s_["t3"][0:1, 0:1], s_["t1"][0:1, 0:1]))
        bcast128([s_["t1"], s_["t3"]], [s_["s4b"], s_["inv4b"]])
        barrier()

        if phase < 8:
            return _finish(nc, steps, sem, A, W, e)
        W("sync", sem["sp0"], sem["sp0"].n)
        W("sync", sem["sp1"], sem["sp1"].n)
        W("sync", sem["so0"], sem["so0"].n)
        W("sync", sem["so1"], sem["so1"].n)
        in_done = []
        oo_done = []
        sgd = []
        v_prev = []
        for it in range(24):
            min_ = fst[it % 2]
            uin = fst[2 + it % 2]
            oo = fst[5 + it % 2]
            lm = sem["lm0"] if it % 2 == 0 else sem["lm1"]
            so = sem["so0"] if it % 2 == 0 else sem["so1"]
            if it >= 2:
                A("sync", lambda E, h=sem["vd"].h, v=in_done[it - 2]: E.wait_ge(h, v))
                A("sync", lambda E, h=sem["sd"].h, v=sgd[it - 2]: E.wait_ge(h, v))
            dma_in(m_d.ap()[:, it * TOK:(it + 1) * TOK], min_[:, :], sm=lm)
            lv = dma_in(upraw_d.ap()[:, it * TOK:(it + 1) * TOK], uin[:, :], sm=lm)
            W("scalar", lm, lv)
            if it >= 1:
                W("scalar", sem["vd"], v_prev[it - 1])
            A("scalar", lambda E, min_=min_: E.activation(
                fst[4][:, :], min_[:, :], AF.Identity,
                bias=s_["magb"][:, :], scale=s_["inv4b"][:, :]))
            A("scalar", lambda E: E.drain())
            A("scalar", lambda E: E.activation(
                fst[4][:, :], fst[4][:, :], AF.Identity,
                bias=s_["nmagb"][:, :], scale=1.0))
            A("scalar", lambda E: E.drain())
            A("scalar", lambda E: E.activation(
                fst[4][:, :], fst[4][:, :], AF.Identity,
                bias=0.0, scale=s_["s4b"][:, :]))
            sgd.append(s_done())
            W("vector", lm, lv)
            if it >= 2:
                A("vector", lambda E, h=so.h, v=oo_done[it - 2]: E.wait_ge(h, v))
            v_op(lambda E, uin=uin: E.tensor_scalar(fst[7][:, :], uin[:, :], s_["inv2b"][:, :],
                                                    MAGIC, ALU.mult, ALU.add))
            in_done.append(v_done())
            v_op(lambda E: E.tensor_scalar(fst[7][:, :], fst[7][:, :], 1.0, -MAGIC, ALU.mult, ALU.add))
            vdrain()
            v_op(lambda E: E.tensor_scalar(fst[7][:, :], fst[7][:, :], s_["s2b"][:, :], None, ALU.mult))
            vdrain()
            W("vector", sem["sd"], sgd[it])
            v_op(lambda E, oo=oo: E.tensor_tensor(oo[:, :], fst[4][:, :], fst[7][:, :], ALU.mult))
            vdrain()
            v_op(lambda E, oo=oo: E.tensor_reduce(s_["t1"][:, :], oo[:, :], AX.X, ALU.max,
                                                  apply_absolute_value=True))
            vdrain()
            v_op(lambda E: E.tensor_tensor(s_["run_o"][:, :], s_["run_o"][:, :], s_["t1"][:, :], ALU.max))
            v_o = v_done_f()
            v_prev.append(v_o)
            W("gpsimd", sem["vd"], v_o)
            dma_spill(oraw_d.ap()[:, it * TOK:(it + 1) * TOK], oo[:, :], sm=so)
            oo_done.append(so.n)

        v_op(lambda E: E.memset(cc_sb[0:1, :], 0.0))
        vdrain()
        xpart_max(s_["run_o"], 0)
        v_cc = v_done()
        W("gpsimd", sem["vd"], v_cc)
        g_ar = allreduce(3)
        W("vector", sem["gio"], g_ar)
        v_op(lambda E: E.tensor_scalar(s_["t1"][0:1, 0:1], cc_sb2[0:1, 0:1], 1.0 / QMAX, 1e-12,
                                       ALU.mult, ALU.max))
        vdrain()
        v_op(lambda E: E.reciprocal(s_["t3"][0:1, 0:1], s_["t1"][0:1, 0:1]))
        bcast128([s_["t1"], s_["t3"]], [s_["s5b"], s_["inv5b"]])
        barrier()

        if phase < 9:
            return _finish(nc, steps, sem, A, W, e)
        W("sync", sem["so0"], sem["so0"].n)
        W("sync", sem["so1"], sem["so1"].n)
        for c in range(24):
            dma_in(wdownT_d.ap()[:, c * D:(c + 1) * D], wdT_sb[:, c * D:(c + 1) * D])
        for c in range(8):
            dma_in(woT_d.ap()[:, c * D:(c + 1) * D], woT_sb[:, c * D:(c + 1) * D])
        l_wd = sem["lds"].n

        n_out = 0
        so_hist = []
        for half in range(2):
            h0 = half * 1024
            oh_done = []
            oh2_done = []
            W("sync", sem["vd"], sem["vd"].n)
            W("sync", sem["sd"], sem["sd"].n)
            for it in range(24):
                oin = ghalf[:, (it % 2) * 1024:(it % 2 + 1) * 1024]
                lo = sem["lo0"] if it % 2 == 0 else sem["lo1"]
                if it >= 2:
                    A("sync", lambda E, h=sem["sd"].h, v=oh_done[it - 2]: E.wait_ge(h, v))
                lv = dma_in(oraw_d.ap()[:, it * TOK + h0: it * TOK + h0 + 1024], oin,
                            sm=lo)
                W("scalar", lo, lv)
                if it >= 1:
                    W("scalar", sem["vd"], oh2_done[it - 1])
                A("scalar", lambda E, oin=oin: E.activation(
                    sig_scr[:, 0:1024], oin, AF.Identity,
                    bias=s_["magb"][:, :], scale=s_["inv5b"][:, :]))
                s_oh = s_done()
                oh_done.append(s_oh)
                W("vector", sem["sd"], s_oh)
                v_op(lambda E, it=it: E.tensor_scalar(ohat_sb[:, it * 1024:(it + 1) * 1024],
                                                      sig_scr[:, 0:1024], 1.0, -MAGIC,
                                                      ALU.mult, ALU.add))
                oh2_done.append(v_done_f())
            v_oh = v_done_f()
            W("tensor", sem["vd"], v_oh)
            W("tensor", sem["lds"], l_wd)
            for o in range(8):
                for j in range(2):
                    ps = psA if j == 0 else psB
                    for c in range(24):
                        A("tensor", lambda E, o=o, c=c, j=j, ps=ps: E.matmul(
                            ps[:, 0:512], wdT_sb[:, c * D + o * 128: c * D + (o + 1) * 128],
                            ohat_sb[:, c * 1024 + j * 512: c * 1024 + (j + 1) * 512],
                            start=(c == 0), stop=(c == 23)))
                t_dn = t_done()
                W("scalar", sem["td"], t_dn)
                for j in range(2):
                    ps = psA if j == 0 else psB
                    A("scalar", lambda E, j=j, ps=ps: E.activation(
                        sig_scr[:, j * 512:(j + 1) * 512], ps[:, 0:512], AF.Sigmoid,
                        bias=0.0, scale=s_["s5b"][:, :]))
                s_sg = s_done()
                W("vector", sem["sd"], s_sg)
                for j in range(2):
                    ps = psA if j == 0 else psB
                    v_op(lambda E, o=o, j=j, ps=ps: E.scalar_tensor_tensor(
                        ghalf[:, o * 1024 + j * 512: o * 1024 + (j + 1) * 512],
                        ps[:, 0:512], s_["s5b"][:, :], sig_scr[:, j * 512:(j + 1) * 512],
                        ALU.mult, ALU.mult))
                v_gh = v_done()
                W("tensor", sem["vd"], v_gh)
            for qtr in range(2):
                t0 = h0 + qtr * 512
                if half == 0 and qtr == 0:
                    W("sync", sem["ac0"], sem["ac0"].n)
                    W("sync", sem["ac1"], sem["ac1"].n)
                for c in range(8):
                    dma_in(acat_d.ap()[:, c * TOK + t0: c * TOK + t0 + 512],
                           acat_q[:, c * 512:(c + 1) * 512], sm=sem["lq"])
                for f in range(8):
                    dma_in(xn32_d.ap()[:, f * TOK + t0: f * TOK + t0 + 512],
                           xnq_q[:, f * 512:(f + 1) * 512], sm=sem["lq"])
                l_q = sem["lq"].n
                W("tensor", sem["lq"], l_q)
                vo_prev = None
                for o in range(8):
                    oc = (o % 2) * 512
                    if len(so_hist) >= 2:
                        W("tensor", sem["sd"], so_hist[-2])
                    for c in range(8):
                        A("tensor", lambda E, o=o, c=c, oc=oc: E.matmul(
                            psC[:, oc:oc + 512], woT_sb[:, c * D + o * 128: c * D + (o + 1) * 128],
                            acat_q[:, c * 512:(c + 1) * 512], start=(c == 0), stop=(c == 7)))
                    t_op = t_done()
                    W("scalar", sem["td"], t_op)
                    if vo_prev is not None:
                        W("scalar", sem["vd"], vo_prev)
                    A("scalar", lambda E, o=o, oc=oc: E.activation(
                        ot_scr[:, 0:512], psC[:, oc:oc + 512], AF.Identity,
                        bias=bcol["o"][:, o:o + 1], scale=1.0))
                    s_ot = s_done()
                    so_hist.append(s_ot)
                    W("vector", sem["sd"], s_ot)
                    v_op(lambda E, o=o: E.scalar_tensor_tensor(
                        ot_scr[:, 512:1024], ot_scr[:, 0:512], s_["rzb"][:, :],
                        xnq_q[:, o * 512:(o + 1) * 512], ALU.mult, ALU.add))
                    vdrain()
                    v_op(lambda E, o=o, qtr=qtr: E.tensor_tensor(
                        wgq[:, o * 512:(o + 1) * 512], ot_scr[:, 512:1024],
                        ghalf[:, o * 1024 + qtr * 512: o * 1024 + qtr * 512 + 512], ALU.mult))
                    vo_prev = v_done_f()
                v_wgq = v_done_f()
                W("tensor", sem["vd"], v_wgq)
                for tb in range(4):
                    lx = dma_in(xq_d.ap()[t0 + tb * 128: t0 + (tb + 1) * 128, :], xq_tok[:, :],
                                sm=sem["lx"])
                    for f in range(8):
                        A("tensor", lambda E, tb=tb, f=f: E.transpose(
                            psD[:, f * 128:(f + 1) * 128],
                            wgq[:, f * 512 + tb * 128: f * 512 + (tb + 1) * 128], ident[:, :]))
                    t_tr = t_done()
                    W("vector", sem["td"], t_tr)
                    W("vector", sem["lx"], lx)
                    if n_out >= 1:
                        W("vector", sem["st"], sem["st"].n)
                    v_op(lambda E: E.tensor_tensor(out_sb[:, :], psD[:, :], xq_tok[:, :], ALU.add))
                    v_out = v_done_f()
                    W("tensor", sem["vd"], v_out)
                    W("sync", sem["vd"], v_out)
                    A("sync", lambda E, t0=t0, tb=tb, h=sem["st"].h: E.dma_start(
                        out=y_d.ap()[t0 + tb * 128: t0 + (tb + 1) * 128, :],
                        in_=out_sb[:, :]).then_inc(h, 16))
                    sem["st"].inc(16)
                    n_out += 1
                    A("sync", lambda E, h=sem["st"].h, v=sem["st"].n: E.wait_ge(h, v))

        W("sync", sem["st"], sem["st"].n)
        return _finish(nc, steps, sem, A, W, e)


def _finish(nc, steps, sem, A, W, e):
    with nc.Block() as block:
        def mk_replay(name):
            def f(E):
                for eng2, fn in steps:
                    if eng2 == name:
                        fn(E)
            return f
        for name in ENGINES:
            getattr(block, name)(mk_replay(name))

    return nc


def kernel(**inputs):
    x = np.ascontiguousarray(inputs["L0_attn_x"], dtype=np.float32)
    in_maps = []
    for c in range(8):
        b, half = c // 2, c % 2
        m = {
            "xq": np.ascontiguousarray(x[b, half * TOK:(half + 1) * TOK]),
            "xk0": np.ascontiguousarray(x[b, 0:TOK]),
            "xk1": np.ascontiguousarray(x[b, TOK:SEQ]),
            "rez": np.ascontiguousarray(inputs["L0_attn_rezero"], dtype=np.float32),
            "w_gate": np.ascontiguousarray(inputs["L0_ffn_w_gate"], dtype=np.float32),
            "w_up": np.ascontiguousarray(inputs["L0_ffn_w_up"], dtype=np.float32),
            "w_down": np.ascontiguousarray(inputs["L0_ffn_w_down"], dtype=np.float32),
        }
        for p in "qkvo":
            m["W" + p] = np.ascontiguousarray(inputs[f"L0_attn_W{p}"], dtype=np.float32)
            m["b" + p] = np.ascontiguousarray(inputs[f"L0_attn_b{p}"], dtype=np.float32)
            m["A" + p] = np.ascontiguousarray(inputs[f"L0_attn_A{p}"], dtype=np.float32)
            m["B" + p] = np.ascontiguousarray(inputs[f"L0_attn_B{p}"], dtype=np.float32)
        in_maps.append(m)
    nc = build(8)
    res = run_bass_kernel_spmd(nc, in_maps, core_ids=list(range(8)))
    out = np.empty((4, SEQ, D), dtype=np.float32)
    for c in range(8):
        b, half = c // 2, c % 2
        out[b, half * TOK:(half + 1) * TOK] = res.results[c]["y"]
    return out
